# version 03 (final)
# baseline (speedup 1.0000x reference)
import math
import numpy as np
import ml_dtypes
from contextlib import ExitStack
import concourse.bass as bass
import concourse.mybir as mybir
from concourse.bass_utils import run_bass_kernel_spmd

F32 = mybir.dt.float32
BF16 = mybir.dt.bfloat16
F16 = mybir.dt.float16
I32 = mybir.dt.int32
AF = mybir.ActivationFunctionType
ALU = mybir.AluOpType
AX = mybir.AxisListType
BF = ml_dtypes.bfloat16

T = 4096
D = 1024
NT = T // 128
NC = D // 128
NQ = T // 512
MAGIC = 12582912.0

N_PHASES = 99
TAPS = []
FFT_STEPS = 6


def host_constants():
    c = {}
    c["ident_f32"] = np.eye(128, dtype=np.float32)
    c["ident_bf16"] = np.eye(128, dtype=BF)
    t = np.arange(T, dtype=np.float64)

    i32v = np.arange(32, dtype=np.float64)
    inv = 1.0 / (10000.0 ** (2 * i32v / 64.0))
    fr = t[:, None] * inv[None, :]
    cos = np.cos(fr).T.astype(np.float32)
    sin = np.sin(fr).T.astype(np.float32)
    cs = np.zeros((128, T), dtype=np.float32)
    sn = np.zeros((128, T), dtype=np.float32)
    for b0 in (0, 64):
        cs[b0:b0 + 32] = cos
        cs[b0 + 32:b0 + 64] = cos
        sn[b0:b0 + 32] = -sin
        sn[b0 + 32:b0 + 64] = sin
    c["rope_cos"] = cs
    c["rope_sin"] = sn

    ih = np.arange(0, 64, 2, dtype=np.float64)
    invf = 1.0 / (10000.0 ** (ih / 64.0))
    fr1 = t[:, None] * invf[None, :]
    base_ = (ih + 0.4 * 64) / (1.4 * 64)
    power = (t - T // 2) / 256.0
    scl = base_[None, :] ** power[:, None]
    cosx = np.cos(fr1)
    sinx = np.sin(fr1)

    def pack_xpos(ct, st):
        C_ = np.zeros((128, T), dtype=np.float32)
        S_ = np.zeros((128, T), dtype=np.float32)
        for hb in (0, 64):
            C_[hb:hb + 32] = ct.T
            C_[hb + 32:hb + 64] = ct.T
            S_[hb:hb + 32] = -st.T
            S_[hb + 32:hb + 64] = st.T
        return C_, S_
    c["xq_cos"], c["xq_sin"] = pack_xpos(cosx * scl, sinx * scl)
    c["xk_cos"], c["xk_sin"] = pack_xpos(cosx / scl, sinx / scl)

    n1v = np.arange(32)
    k1v = np.arange(64)
    n2v = np.arange(128)
    k2v = np.arange(128)
    F64 = np.exp(-2j * np.pi * np.outer(n1v, k1v) / 64.0)
    c["f64c"] = np.real(F64).astype(BF)
    c["f64s"] = np.imag(F64).astype(BF)
    tw = np.exp(-2j * np.pi * np.outer(k1v, n2v) / 8192.0)
    c["twc"] = np.real(tw).astype(np.float32)
    c["tws"] = np.imag(tw).astype(np.float32)
    c["twsn"] = (-np.imag(tw)).astype(np.float32)
    F128 = np.exp(-2j * np.pi * np.outer(n2v, k2v) / 128.0)
    c["f128c"] = np.real(F128).astype(BF)
    c["f128s"] = np.imag(F128).astype(BF)
    c["f128sn"] = (-np.imag(F128)).astype(BF)
    IF128 = np.exp(2j * np.pi * np.outer(k2v, n2v) / 128.0)
    c["i128c"] = np.real(IF128).astype(BF)
    c["i128s"] = np.imag(IF128).astype(BF)
    c["i128sn"] = (-np.imag(IF128)).astype(BF)
    itw = np.exp(2j * np.pi * np.outer(n2v, k1v) / 8192.0) / 8192.0
    c["itwc"] = np.real(itw).astype(np.float32)
    c["itws"] = np.imag(itw).astype(np.float32)
    c["itwsn"] = (-np.imag(itw)).astype(np.float32)
    IB = np.exp(2j * np.pi * np.outer(k1v, n1v) / 64.0)
    c["ibc"] = np.real(IB).astype(BF)
    c["ibsn"] = (-np.imag(IB)).astype(BF)

    ci = np.arange(D)
    E = np.exp(-2j * np.pi * np.outer(ci, ci) / D)
    Dm = np.exp(2j * np.pi * np.outer(ci, ci) / D) / D
    c["l2_Er"] = np.real(E).astype(np.float32)
    c["l2_Ei"] = np.imag(E).astype(np.float32)
    c["l2_Dr"] = np.real(Dm).astype(np.float32)
    c["l2_Di"] = np.imag(Dm).astype(np.float32)

    jp = np.arange(128)[:, None]
    ic = np.arange(512)[None, :]

    def mk_mask(keep):
        return keep.astype(BF)
    for r in range(4):
        c[f"m_l2_{r}"] = mk_mask(128 * r + jp <= ic)
    for r in range(-4, 4):
        d = 128 * r + jp - ic
        c[f"m_l1_{r+4}"] = mk_mask((d <= 0) & (d >= -512))
    for r in range(-1, 5):
        d = 128 * r + jp - ic
        c[f"m_l3_{r+1}"] = mk_mask(np.abs(d) <= 64)

    ind2 = np.zeros((128, 2), dtype=np.float32)
    ind2[0:64, 0] = 1
    ind2[64:128, 1] = 1
    c["ind2f"] = ind2
    sel2 = np.zeros((2, 128), dtype=np.float32)
    sel2[0, 0:64] = 1
    sel2[1, 64:128] = 1
    c["sel2f"] = sel2
    return c


HOSTC = host_constants()


class Buf:
    __slots__ = ("t", "w", "readers", "name")

    def __init__(self, t, name=""):
        self.t = t
        self.w = None
        self.readers = []
        self.name = name


class MK:
    def __init__(self):
        self.stack = ExitStack()
        nc = bass.Bass("TRN2", target_bir_lowering=False)
        self.nc = nc
        self.eng = {"t": nc.tensor, "v": nc.vector, "s": nc.scalar,
                    "g": nc.gpsimd, "q": nc.sync}
        self.esem = {k: self.stack.enter_context(nc.semaphore(f"se_{k}"))
                     for k in self.eng}
        self.ecount = {k: 0 for k in self.eng}
        self.droles = [r + q for r in ("w", "x", "o", "m") for q in ("q", "g")]
        self.dsem = {r: self.stack.enter_context(nc.semaphore(f"sd_{r}"))
                     for r in self.droles}
        self.dcount = {r: 0 for r in self.droles}
        self.waited = {k: {} for k in self.eng}
        self.cp_slots = {}
        self.cp_next = 0
        self.bias_slots = {}
        self.bias_next = 0
        self.declare()

    def _wait(self, ek, sem, val):
        if val <= 0:
            return
        key = id(sem)
        if self.waited[ek].get(key, -1) >= val:
            return
        self.eng[ek].wait_ge(sem, val)
        self.waited[ek][key] = val

    def _resolve(self, tok):
        if tok[0] == "e":
            return self.esem[tok[1]], tok[2]
        return self.dsem[tok[1]], 16 * self.dcount[tok[1]]

    def _do_waits(self, ek, reads, writes):
        toks = []
        for b in reads:
            if b is not None and b.w is not None:
                toks.append(b.w)
        for b in writes:
            if b is None:
                continue
            if b.w is not None:
                toks.append(b.w)
            toks.extend(b.readers)
        best = {}
        for tk in toks:
            if tk[0] == "e" and tk[1] == ek and ek == "t":
                continue
            sem, val = self._resolve(tk)
            k = id(sem)
            if k not in best or best[k][1] < val:
                best[k] = (sem, val)
        for sem, val in best.values():
            self._wait(ek, sem, val)

    def _apply(self, tok, reads, writes):
        for b in reads:
            if b is not None:
                b.readers.append(tok)
        for b in writes:
            if b is not None:
                b.w = tok
                b.readers = []

    def op(self, ek, meth, *args, reads=(), writes=(), **kw):
        self._do_waits(ek, reads, writes)
        inst = getattr(self.eng[ek], meth)(*args, **kw)
        self.ecount[ek] += 1
        inst.then_inc(self.esem[ek], 1)
        self._apply(("e", ek, self.ecount[ek]), reads, writes)
        return inst

    def group(self, ek, calls, reads=(), writes=()):
        self._do_waits(ek, reads, writes)
        n = len(calls)
        for j, (meth, args, kw) in enumerate(calls):
            inst = getattr(self.eng[ek], meth)(*args, **kw)
            if j == n - 1:
                self.ecount[ek] += 1
                inst.then_inc(self.esem[ek], 1)
        self._apply(("e", ek, self.ecount[ek]), reads, writes)

    def dma(self, qk, out_ap, in_ap, reads=(), writes=(), role="m", **kw):
        try:
            if out_ap.tensor.dtype != in_ap.tensor.dtype:
                qk = "g"
        except AttributeError:
            pass
        key = role + qk
        self._do_waits(qk, reads, writes)
        inst = self.eng[qk].dma_start(out_ap, in_ap, **kw)
        self.dcount[key] += 1
        inst.then_inc(self.dsem[key], 16)
        self._apply(("d", key), reads, writes)
        return inst

    def declare(self):
        nc = self.nc
        st = self.stack

        def din(name, shape, dt=F32):
            return Buf(nc.dram_tensor(name, list(shape), dt,
                                      kind="ExternalInput").ap(), name)

        def dscr(name, shape, dt=F32):
            return Buf(nc.dram_tensor(name, list(shape), dt,
                                      kind="Internal").ap(), name)

        def sbuf(name, shape, dt):
            return Buf(st.enter_context(nc.sbuf_tensor(name, list(shape), dt)),
                       name)

        def psum(name, shape, dt=F32):
            return Buf(st.enter_context(nc.psum_tensor(name, list(shape), dt)),
                       name)

        self.X_in = din("x", (T, D))
        self.W = {}
        for nm, shp, dt in [
            ("L0_attn_Wq", (D, D), F32), ("L0_attn_bq", (D,), F32),
            ("L0_attn_Wl", (D, 256), F32), ("L0_attn_bl", (256,), F32),
            ("L0_attn_Wk", (256, D), F32), ("L0_attn_bk", (D,), F32),
            ("L0_attn_Wv", (256, D), F32), ("L0_attn_bv", (D,), F32),
            ("L0_attn_Wo", (D, D), F32), ("L0_attn_bo", (D,), F32),
            ("L0_ffn_clip_x", (1,), F32), ("L0_ffn_clip_down", (1,), F32),
            ("L0_ffn_w_gate", (2816, D), F32), ("L0_ffn_w_up", (2816, D), F32),
            ("L0_ffn_w_down", (D, 2816), F32),
            ("L1_attn_t", (T, D), F32),
            ("L1_ffn_wqkv", (2, D, 1536), F32),
            ("L1_ffn_q_scale", (2, 64), F32), ("L1_ffn_k_scale", (2, 64), F32),
            ("L1_ffn_wo", (2, 512, D), F32),
            ("L1_ffn_ln_g", (2, D), F32), ("L1_ffn_ln_b", (2, D), F32),
            ("L1_ffn_w1", (2, D, 5632), F32), ("L1_ffn_b1", (2, 5632), F32),
            ("L1_ffn_w2", (2, 2816, D), F32), ("L1_ffn_b2", (2, D), F32),
            ("L2_attn_qw", (D, D), F16), ("L2_attn_kw", (D, D), F16),
            ("L2_attn_vw", (D, D), F16), ("L2_attn_ow", (D, D), F16),
            ("L2_ffn_w_freq_real", (D, D), F32),
            ("L2_ffn_w_freq_imag", (D, D), F32),
            ("L2_ffn_bias", (D,), F32),
            ("L3_attn_w_qkv", (D, 3 * D), F32), ("L3_attn_b_qkv", (3 * D,), F32),
            ("L3_attn_w_out", (D, D), F32), ("L3_attn_b_out", (D,), F32),
            ("L3_ffn_W_r", (D, D), F32), ("L3_ffn_W_k", (D, D), F32),
            ("L3_ffn_W_v", (D, D), F32),
            ("L3_ffn_mu_r", (D,), F32), ("L3_ffn_mu_k", (D,), F32),
        ]:
            self.W[nm] = din(nm, shp, dt)
        self.C = {}
        for nm, arr in HOSTC.items():
            dt = BF16 if arr.dtype == BF else F32
            self.C[nm] = din("c_" + nm, arr.shape, dt)

        self.OUT = Buf(nc.dram_tensor("out", [T, D], F32,
                                      kind="ExternalOutput").ap(), "out")

        self.XT = dscr("XT", (D, T))
        self.XNT = dscr("XNT", (D, T))
        self.XNtokB = dscr("XNtokB", (T, D), BF16)
        self.ST = dscr("ST", (D, T))
        self.GT = dscr("GT", (D, T))
        self.Stok = dscr("Stok", (T, D))
        self.QTh = dscr("QTh", (D, T), BF16)
        self.KTh = dscr("KTh", (D, T), BF16)
        self.Vh = dscr("Vh", (T, D), BF16)
        self.OTh = dscr("OTh", (D, T), BF16)
        self.LATh = dscr("LATh", (256, T), BF16)
        self.KXT = dscr("KXT", (D, T), BF16)
        self.KWGT = dscr("KWGT", (D, 2816), BF16)
        self.KWUT = dscr("KWUT", (D, 2816), BF16)
        self.KDT = dscr("KDT", (2816, D), BF16)
        self.Hh = dscr("Hh", (2816, T))
        self.KHT = dscr("KHT", (2816, T), BF16)
        self.Gr = dscr("Gr", (128, 64 * D))
        self.Gi = dscr("Gi", (128, 64 * D))
        self.TFr = dscr("TFr", (128, 64 * D))
        self.TFi = dscr("TFi", (128, 64 * D))
        self.H2r = dscr("H2r", (64, 128 * D))
        self.H2i = dscr("H2i", (64, 128 * D))
        self.XIT = dscr("XIT", (D, T))
        self.HSH = dscr("HSH", (D, T), BF16)
        self.HSHF = dscr("HSHF", (D, T))
        self.WQB = dscr("WQB", (D, D), BF16)
        self.WKB = dscr("WKB", (D, D), BF16)
        self.WVB = dscr("WVB", (D, D), BF16)
        self.WOB = dscr("WOB", (D, D), BF16)
        self.ATr = dscr("ATr", (D, D))
        self.ATi = dscr("ATi", (D, D))
        self.Mh = dscr("Mh", (D, D), BF16)
        self.WRT = dscr("WRT", (D, D), BF16)
        self.WKT = dscr("WKT", (D, D), BF16)
        self.WVT = dscr("WVT", (D, D), BF16)
        self.XRT = dscr("XRT", (D, T), BF16)
        self.XKT2 = dscr("XKT2", (D, T), BF16)
        self.RTf = dscr("RTf", (D, T))
        self.KATh = dscr("KATh", (D, T), BF16)
        self.cci_gx = dscr("cci_gx", (1, 8))
        self.cco_gx = dscr("cco_gx", (1, 8))
        self.cci_gh = dscr("cci_gh", (1, 8))
        self.cco_gh = dscr("cco_gh", (1, 8))
        self.DBG = dscr("DBG", (128, 64))

        self.A1 = sbuf("A1", (128, 8192), BF16)
        self.A2 = sbuf("A2", (128, 34816), BF16)
        self.A5 = sbuf("A5", (128, 8192), BF16)
        self.OTB = sbuf("OTB", (128, 4096), BF16)
        self.XS = sbuf("XS", (128, 4096), F32)
        self.WB = sbuf("WB", (128, 8192), BF16)
        self.WF = sbuf("WF", (128, 2048), F32)
        self.TRIG = sbuf("TRIG", (128, 2048), F32)
        self.MSK = sbuf("MSK", (128, 4096), BF16)
        self.SM = sbuf("SM", (128, 1024), F32)
        self.QSW = sbuf("QSW", (128, 1024), F32)
        self.SEL = sbuf("SEL", (128, 256), F32)
        self.HF = sbuf("HF", (128, 1024), F16)
        self.FCB = sbuf("FCB", (128, 1024), BF16)
        self.FCF = sbuf("FCF", (128, 768), F32)
        self.CP = sbuf("CP", (128, 64), F32)
        self.BIAS = sbuf("BIAS", (128, 128), F32)
        self.BIAS2 = sbuf("BIAS2", (128, 64), F32)
        self.ID32 = sbuf("ID32", (128, 128), F32)
        self.IDB = sbuf("IDB", (128, 128), BF16)
        self.ONER = sbuf("ONER", (1, 128), F32)
        self.ONEC = sbuf("ONEC", (128, 1), F32)
        self.ONEJB = sbuf("ONEJB", (128, 64), BF16)
        self.ONE512B = sbuf("ONE512B", (1, 512), BF16)

        self.PSF = [psum(f"psf{i}", (128, 512), F32) for i in range(6)]
        self.PSB = [psum(f"psb{i}", (128, 512), BF16) for i in range(2)]
        self._psf_i = 0
        self._psb_i = 0
        self._pins = set()

        self.dma("q", self.ID32.t[:], self.C["ident_f32"].t,
                 writes=(self.ID32,))
        self.dma("q", self.IDB.t[:], self.C["ident_bf16"].t,
                 writes=(self.IDB,))
        self.op("v", "memset", self.ONER.t[:], 1.0, writes=(self.ONER,))
        self.op("v", "memset", self.ONEC.t[:], 1.0, writes=(self.ONEC,))
        self.op("v", "memset", self.ONEJB.t[:], 1.0, writes=(self.ONEJB,))
        self.op("v", "memset", self.ONE512B.t[:], 1.0, writes=(self.ONE512B,))
        self.op("v", "memset", self.CP.t[:], 0.0, writes=(self.CP,))

    def npsf(self):
        for _ in range(len(self.PSF)):
            b = self.PSF[self._psf_i % len(self.PSF)]
            self._psf_i += 1
            if id(b) not in self._pins:
                return b
        raise RuntimeError("all psf pinned")

    def npsb(self):
        b = self.PSB[self._psb_i % len(self.PSB)]
        self._psb_i += 1
        return b

    def pin(self, b):
        self._pins.add(id(b))

    def unpin(self, b):
        self._pins.discard(id(b))

    def cp(self, key):
        if key not in self.cp_slots:
            self.cp_slots[key] = self.cp_next
            self.cp_next += 1
            assert self.cp_next <= 64
        i = self.cp_slots[key]
        return self.CP.t[:, i:i + 1]

    def cpc(self, val):
        key = ("const", float(val))
        if key not in self.cp_slots:
            ap = self.cp(key)
            self.op("v", "memset", ap, float(val), writes=(self.CP,))
            return ap
        return self.cp(key)

    def bcast11(self, src_ap, key):
        pb = self.npsf()
        self.op("t", "matmul", pb.t[:, 0:1], self.ONER.t[:], src_ap,
                start=True, stop=True, reads=(self.SM, self.ONER),
                writes=(pb,))
        self.op("v", "tensor_copy", self.cp(key), pb.t[:, 0:1],
                reads=(pb,), writes=(self.CP,))

    def part_max(self, col_ap, dst_ap):
        st = self.WF.t[:, 0:128]
        self.op("v", "memset", st, -1e30, writes=(self.WF,))
        self.op("v", "tensor_copy", st[:, 0:1], col_ap, reads=(self.XS,),
                writes=(self.WF,))
        pb = self.npsf()
        self.op("t", "transpose", pb.t[:, 0:128], st, self.ID32.t[:],
                reads=(self.WF, self.ID32), writes=(pb,))
        self.op("v", "reduce_max", dst_ap, pb.t[0:1, 0:128], AX.X,
                reads=(pb,), writes=(self.SM,))

    def collective_post(self, src_ap, name):
        cci = getattr(self, "cci_" + name)
        cco = getattr(self, "cco_" + name)
        self.op("v", "memset", self.SM.t[0:1, 8:16], -1e30, writes=(self.SM,))
        self.op("v", "tensor_copy", self.SM.t[0:1, 8:9], src_ap,
                writes=(self.SM,))
        self.dma("g", cci.t[:], self.SM.t[0:1, 8:16],
                 reads=(self.SM,), writes=(cci,))
        self.op("g", "collective_compute", "AllReduce", ALU.max,
                [[0, 1, 2, 3, 4, 5, 6, 7]], [cci.t[:]], [cco.t[:]],
                reads=(cci,), writes=(cco,))

    def collective_read(self, name, key):
        cco = getattr(self, "cco_" + name)
        self.dma("g", self.SM.t[0:1, 8:16], cco.t[:],
                 reads=(cco,), writes=(self.SM,))
        self.bcast11(self.SM.t[0:1, 8:9], key)

    def load_bias_cols(self, wbuf, n, key):
        nch = (n + 127) // 128
        off = self.bias_slot_alloc(key, nch)
        ap = wbuf.t.rearrange("(c p) -> p c", p=128)
        self.dma("q", self.BIAS.t[:, off:off + nch], ap, reads=(wbuf,),
                 writes=(self.BIAS,), allow_slow_non_contiguous=True)
        return off

    def bias_col(self, key, c):
        o = self.bias_slots[key]
        return self.BIAS.t[:, o + c:o + c + 1]

    def bias_slot_alloc(self, key, nch):
        if key not in self.bias_slots:
            self.bias_slots[key] = self.bias_next
            self.bias_next += nch
            assert self.bias_next <= 128
        return self.bias_slots[key]

    def evac_f32(self, dst_h, pb, oc, tb, func=None, scale=None,
                 scale_buf=None):
        sl = self.XS.t[:, 3584:4096]
        if func is None and scale is None:
            self.op("v", "tensor_copy", sl, pb.t[:, 0:512],
                    reads=(pb,), writes=(self.XS,))
        else:
            kw = {}
            if scale is not None:
                kw["scale"] = scale
            self.op("s", "activation", sl, pb.t[:, 0:512],
                    func or AF.Identity,
                    reads=(pb,) + ((scale_buf,) if scale_buf else ()),
                    writes=(self.XS,), **kw)
        self.dma("q", dst_h.t[oc * 128:(oc + 1) * 128,
                              tb * 512:(tb + 1) * 512],
                 sl, reads=(self.XS,), writes=(dst_h,), role="o")

    def evac_bf16(self, dst_h, pb, oc, tb, func=None, scale=None,
                  rows_off=0):
        sl = self.OTB.t[:, 3584:4096]
        if func is None and scale is None:
            self.op("v", "tensor_copy", sl, pb.t[:, 0:512],
                    reads=(pb,), writes=(self.OTB,))
        else:
            kw = {}
            if scale is not None:
                kw["scale"] = scale
            self.op("s", "activation", sl, pb.t[:, 0:512],
                    func or AF.Identity, reads=(pb, self.CP),
                    writes=(self.OTB,), **kw)
        self.dma("q", dst_h.t[rows_off + oc * 128:rows_off + (oc + 1) * 128,
                              tb * 512:(tb + 1) * 512],
                 sl, reads=(self.OTB,), writes=(dst_h,), role="o")

    def proj(self, src_h, w_buf, w_ap, n_in, n_out, bias_buf, bias_ap, evac,
             stage_rhs=None):
        nin_c = n_in // 128
        nout_c = n_out // 128
        assert nin_c <= 8 and n_out <= 3072
        if bias_ap is not None:
            self.dma("q", self.WB.t[0:1, 4096:4096 + n_out], bias_ap,
                     reads=(bias_buf,), writes=(self.WB,))
        for tb in range(NQ):
            if stage_rhs is not None:
                stage_rhs(tb)
            else:
                for c in range(nin_c):
                    src_ap = src_h.t[c * 128:(c + 1) * 128,
                                     tb * 512:(tb + 1) * 512]
                    self.dma("g", self.A5.t[:, c * 512:(c + 1) * 512], src_ap,
                             reads=(src_h,), writes=(self.A5,), role="x")
            for oc in range(nout_c):
                wslot = (oc % 2) * 2048
                wcols = w_ap[:, oc * 128:(oc + 1) * 128]
                wcols = wcols.rearrange("(c p) o -> p c o", p=128)
                self.dma("q",
                         self.WB.t[:, wslot:wslot + nin_c * 128].rearrange(
                             "p (c o) -> p c o", o=128),
                         wcols, reads=(w_buf,), writes=(self.WB,), role="w")
                pb = self.npsf()
                calls = []
                for c in range(nin_c):
                    calls.append(("matmul",
                                  (pb.t[:, 0:512],
                                   self.WB.t[:, wslot + c * 128:wslot + (c + 1) * 128],
                                   self.A5.t[:, c * 512:(c + 1) * 512]),
                                  dict(start=(c == 0),
                                       stop=(c == nin_c - 1 and bias_ap is None))))
                if bias_ap is not None:
                    calls.append(("matmul",
                                  (pb.t[:, 0:512],
                                   self.WB.t[0:1, 4096 + oc * 128:4096 + (oc + 1) * 128],
                                   self.ONE512B.t[:]),
                                  dict(start=False, stop=True)))
                self.group("t", calls,
                           reads=(self.WB, self.A5, self.ONE512B),
                           writes=(pb,))
                evac(pb, oc, tb)

    def attn_core(self, qt_h, kt_h, v_h, ot_h, nh, scale_key, jts, maskof):
        inner = nh * 64
        for jc in range(NT):
            self.dma("g", self.A2.t[:, jc * inner:(jc + 1) * inner],
                     v_h.t[jc * 128:(jc + 1) * 128, 0:inner],
                     reads=(v_h,), writes=(self.A2,), role="x")
        for c in range(nh // 2):
            self.dma("q", self.A1.t[:, 0:4096],
                     kt_h.t[c * 128:(c + 1) * 128, :],
                     reads=(kt_h,), writes=(self.A1,), role="x")
            self.dma("q", self.A5.t[:, 0:4096],
                     qt_h.t[c * 128:(c + 1) * 128, :],
                     reads=(qt_h,), writes=(self.A5,), role="x")
            for hh in range(2):
                h = 2 * c + hh
                half = 64 * hh
                for q in range(NQ):
                    jlist = list(jts(q))
                    po = self.PSF[4]
                    pl = self.PSF[5]
                    self.pin(po)
                    self.pin(pl)
                    for ji, jt in enumerate(jlist):
                        ps = self.npsf()
                        self.op("t", "matmul", ps.t[:, 0:512],
                                self.A1.t[half:half + 64,
                                          jt * 128:(jt + 1) * 128],
                                self.A5.t[half:half + 64,
                                          q * 512:(q + 1) * 512],
                                start=True, stop=True,
                                reads=(self.A1, self.A5), writes=(ps,))
                        pslot = 4096 + (ji % 4) * 512
                        pt = self.A1.t[:, pslot:pslot + 512]
                        self.op("s", "activation", pt, ps.t[:, 0:512],
                                AF.Exp, scale=self.cp(scale_key),
                                reads=(ps, self.CP), writes=(self.A1,))
                        m = maskof(jt, q)
                        if m is not None:
                            self.op("v", "tensor_tensor", pt, pt, m,
                                    ALU.mult, reads=(self.MSK,),
                                    writes=(self.A1,))
                        st = (ji == 0)
                        sp = (ji == len(jlist) - 1)
                        self.op("t", "matmul", po.t[0:64, 0:512],
                                self.A2.t[:, jt * inner + h * 64:
                                          jt * inner + h * 64 + 64],
                                pt, start=st, stop=sp,
                                reads=(self.A2, self.A1), writes=(po,),
                                skip_group_check=True)
                        self.op("t", "matmul", pl.t[0:64, 0:512],
                                self.ONEJB.t[:], pt, start=st, stop=sp,
                                reads=(self.ONEJB, self.A1), writes=(pl,),
                                skip_group_check=True)
                    self.op("v", "reciprocal", self.SM.t[0:64, 0:512],
                            pl.t[0:64, 0:512], reads=(pl,), writes=(self.SM,))
                    osl = self.OTB.t[0:64, 0:512]
                    self.op("v", "tensor_tensor", osl, po.t[0:64, 0:512],
                            self.SM.t[0:64, 0:512], ALU.mult,
                            reads=(po, self.SM), writes=(self.OTB,))
                    self.unpin(po)
                    self.unpin(pl)
                    self.dma("q",
                             ot_h.t[h * 64:h * 64 + 64,
                                    q * 512:(q + 1) * 512],
                             osl, reads=(self.OTB,), writes=(ot_h,),
                             role="o")

    def stage_masks(self, names):
        out = {}
        for i, nm in enumerate(names):
            sl = self.MSK.t[:, i * 512:(i + 1) * 512]
            self.dma("q", sl, self.C[nm].t, reads=(self.C[nm],),
                     writes=(self.MSK,))
            out[nm] = sl
        return out

    def p0_ln(self):
        self.op("v", "memset", self.XS.t[:, 2048:2049], 0.0, writes=(self.XS,))
        for tt in range(NT):
            xs = self.XS.t[:, 0:1024]
            self.dma("q", xs, self.X_in.t[tt * 128:(tt + 1) * 128, :],
                     reads=(self.X_in,), writes=(self.XS,), role="x")
            mu = self.SM.t[:, 0:1]
            sq = self.SM.t[:, 1:2]
            self.op("v", "reduce_sum", mu, xs, AX.X, writes=(self.SM,),
                    reads=(self.XS,))
            self.op("s", "activation", self.WF.t[:, 0:1024], xs, AF.Square,
                    reads=(self.XS,), writes=(self.WF,))
            self.op("v", "reduce_sum", sq, self.WF.t[:, 0:1024], AX.X,
                    reads=(self.WF,), writes=(self.SM,))
            self.op("v", "tensor_scalar", mu, mu, 1.0 / 1024.0, None,
                    ALU.mult, writes=(self.SM,))
            self.op("v", "tensor_scalar", sq, sq, 1.0 / 1024.0, None,
                    ALU.mult, writes=(self.SM,))
            v2 = self.SM.t[:, 2:3]
            self.op("v", "tensor_tensor", v2, mu, mu, ALU.mult,
                    writes=(self.SM,))
            self.op("v", "tensor_tensor", v2, sq, v2, ALU.subtract,
                    writes=(self.SM,))
            self.op("v", "tensor_scalar", v2, v2, 1e-5, None, ALU.add,
                    writes=(self.SM,))
            rs = self.SM.t[:, 3:4]
            self.op("s", "activation", rs, v2, AF.Sqrt, writes=(self.SM,))
            self.op("v", "reciprocal", rs, rs, writes=(self.SM,))
            if tt == 1:
                self.dma("q", self.DBG.t[:, 0:8], self.SM.t[:, 0:8],
                         reads=(self.SM,), writes=(self.DBG,), role="m")
            xn = self.XS.t[:, 1024:2048]
            self.op("v", "tensor_scalar", xn, xs, mu, None, ALU.subtract,
                    reads=(self.SM,), writes=(self.XS,))
            self.op("v", "tensor_scalar", xn, xn, rs, None, ALU.mult,
                    reads=(self.SM,), writes=(self.XS,))
            mx = self.SM.t[:, 4:5]
            self.op("v", "reduce_max", mx, xn, AX.X,
                    apply_absolute_value=True, reads=(self.XS,),
                    writes=(self.SM,))
            self.op("v", "tensor_tensor", self.XS.t[:, 2048:2049],
                    self.XS.t[:, 2048:2049], mx, ALU.max,
                    reads=(self.SM,), writes=(self.XS,))
            for c in range(NC):
                for src, dst in ((xs, self.XT), (xn, self.XNT)):
                    pb = self.npsf()
                    self.op("t", "transpose", pb.t[:, 0:128],
                            src[:, c * 128:(c + 1) * 128], self.ID32.t[:],
                            reads=(self.XS, self.ID32), writes=(pb,))
                    wf = self.WF.t[:, 1024 + (c % 4) * 128:
                                   1024 + (c % 4) * 128 + 128]
                    self.op("v", "tensor_copy", wf, pb.t[:, 0:128],
                            reads=(pb,), writes=(self.WF,))
                    self.dma("q", dst.t[c * 128:(c + 1) * 128,
                                        tt * 128:(tt + 1) * 128],
                             wf, reads=(self.WF,), writes=(dst,), role="o")
        self.part_max(self.XS.t[:, 2048:2049], self.SM.t[0:1, 0:1])
        self.collective_post(self.SM.t[0:1, 0:1], "gx")

    def l0_qkv(self):
        W = self.W
        self.proj(self.XNT, W["L0_attn_Wq"], W["L0_attn_Wq"].t, D, D,
                  W["L0_attn_bq"], W["L0_attn_bq"].t,
                  lambda pb, oc, tb: self.evac_bf16(self.QTh, pb, oc, tb))
        self.proj(self.XNT, W["L0_attn_Wl"], W["L0_attn_Wl"].t, D, 256,
                  W["L0_attn_bl"], W["L0_attn_bl"].t,
                  lambda pb, oc, tb: self.evac_bf16(self.LATh, pb, oc, tb))
        self.proj(self.LATh, W["L0_attn_Wk"], W["L0_attn_Wk"].t, 256, D,
                  W["L0_attn_bk"], W["L0_attn_bk"].t,
                  lambda pb, oc, tb: self.evac_bf16(self.KTh, pb, oc, tb))
        self.l0_v()

    def tok_mm(self, lhs_h, lhs_rows, w_buf, w_ap, n_in, n_out, bias_ap,
               dst_h, evac=None, bias_buf=None):
        nin_c = n_in // 128
        nob = n_out // 512
        wap = w_ap.rearrange("(c p) o -> p c o", p=128)
        self.dma("q",
                 self.WB.t[:, 0:nin_c * n_out].rearrange(
                     "p (c o) -> p c o", o=n_out),
                 wap, reads=(w_buf,), writes=(self.WB,), role="w")
        if bias_ap is not None:
            self.dma("q", self.WB.t[0:1, 7168:7168 + n_out], bias_ap,
                     reads=(bias_buf or w_buf,), writes=(self.WB,))
        for tt in range(NT):
            for c in range(nin_c):
                self.dma("g", self.A5.t[:, c * 128:(c + 1) * 128],
                         lhs_h.t[lhs_rows + c * 128:lhs_rows + (c + 1) * 128,
                                 tt * 128:(tt + 1) * 128],
                         reads=(lhs_h,), writes=(self.A5,), role="x")
            for ob in range(nob):
                pb = self.npsf()
                calls = []
                for c in range(nin_c):
                    calls.append(("matmul",
                                  (pb.t[:, 0:512],
                                   self.A5.t[:, c * 128:(c + 1) * 128],
                                   self.WB.t[:, c * n_out + ob * 512:
                                             c * n_out + (ob + 1) * 512]),
                                  dict(start=(c == 0),
                                       stop=(c == nin_c - 1 and bias_ap is None))))
                if bias_ap is not None:
                    calls.append(("matmul",
                                  (pb.t[:, 0:512],
                                   self.ONE512B.t[0:1, 0:128],
                                   self.WB.t[0:1, 7168 + ob * 512:
                                             7168 + (ob + 1) * 512]),
                                  dict(start=False, stop=True)))
                self.group("t", calls,
                           reads=(self.A5, self.WB, self.ONE512B),
                           writes=(pb,))
                if evac is not None:
                    evac(pb, tt, ob)
                else:
                    sl = self.OTB.t[:, 3584:4096]
                    self.op("v", "tensor_copy", sl, pb.t[:, 0:512],
                            reads=(pb,), writes=(self.OTB,))
                    self.dma("q", dst_h.t[tt * 128:(tt + 1) * 128,
                                          ob * 512:(ob + 1) * 512],
                             sl, reads=(self.OTB,), writes=(dst_h,),
                             role="o")

    def l0_v(self):
        W = self.W
        self.tok_mm(self.LATh, 0, W["L0_attn_Wv"], W["L0_attn_Wv"].t,
                    256, D, W["L0_attn_bv"].t, self.Vh,
                    bias_buf=W["L0_attn_bv"])

    def l0_attn(self):
        self.attn_core(self.QTh, self.KTh, self.Vh, self.OTh, 16, "c0125",
                       lambda q: range(NT), lambda jt, q: None)
        W = self.W
        self.proj(self.OTh, W["L0_attn_Wo"], W["L0_attn_Wo"].t, D, D,
                  W["L0_attn_bo"], W["L0_attn_bo"].t,
                  lambda pb, oc, tb: self.evac_f32(self.ST, pb, oc, tb))

    def l0_ffn_scales(self):
        W = self.W
        self.dma("q", self.SM.t[0:1, 16:17], W["L0_ffn_clip_x"].t,
                 reads=(W["L0_ffn_clip_x"],), writes=(self.SM,))
        self.dma("q", self.SM.t[0:1, 17:18], W["L0_ffn_clip_down"].t,
                 reads=(W["L0_ffn_clip_down"],), writes=(self.SM,))
        self.op("s", "activation", self.SM.t[0:1, 16:18],
                self.SM.t[0:1, 16:18], AF.Sigmoid, writes=(self.SM,))

    def _mk_scale(self, sig_ap, g_key, name):
        self.dma("q", self.DBG.t[:, 8:16] if name == "sx"
                 else self.DBG.t[:, 16:24],
                 self.CP.t[:, 0:8], reads=(self.CP,), writes=(self.DBG,),
                 role="m")
        s = self.SM.t[0:1, 20:21]
        self.op("v", "tensor_tensor", s, sig_ap, self.cp(g_key)[0:1, :],
                ALU.mult, reads=(self.SM, self.CP), writes=(self.SM,))
        self.op("v", "tensor_scalar", s, s, 1.0 / 127.0, None, ALU.mult,
                writes=(self.SM,))
        self.bcast11(s, name)
        si = self.SM.t[0:1, 21:22]
        self.op("v", "reciprocal", si, s, reads=(self.SM,), writes=(self.SM,))
        self.bcast11(si, name + "i")

    def quant_tile(self, dst_ap, src_ap, inv_ap, rbufs, wbufs):
        self.op("v", "tensor_scalar", dst_ap, src_ap, inv_ap, None, ALU.mult,
                reads=rbufs + (self.CP,), writes=wbufs)
        self.op("v", "tensor_scalar", dst_ap, dst_ap, MAGIC, MAGIC,
                ALU.add, ALU.subtract, writes=wbufs)
        self.op("v", "tensor_scalar", dst_ap, dst_ap, -128.0, 127.0,
                ALU.max, ALU.min, writes=wbufs)

    def l0_ffn_qx(self):
        self.collective_read("gx", "gx")
        self._mk_scale(self.SM.t[0:1, 16:17], "gx", "sx")
        for c in range(NC):
            for tb in range(NQ):
                xs = self.XS.t[:, 0:512]
                self.dma("q", xs, self.XNT.t[c * 128:(c + 1) * 128,
                                             tb * 512:(tb + 1) * 512],
                         reads=(self.XNT,), writes=(self.XS,), role="x")
                self.quant_tile(xs, xs, self.cp("sxi"), (self.XS,),
                                (self.XS,))
                bsl = self.A5.t[:, 7680:8192]
                self.op("v", "tensor_copy", bsl, xs, reads=(self.XS,),
                        writes=(self.A5,))
                self.dma("q", self.KXT.t[c * 128:(c + 1) * 128,
                                         tb * 512:(tb + 1) * 512],
                         bsl, reads=(self.A5,), writes=(self.KXT,), role="o")

    def l0_ffn_qw(self, w_buf, ni, nf, dst_h, sw_key):
        nic = ni // 128
        nfb = (nf + 1023) // 1024
        off = self.bias_slot_alloc(sw_key, nic)
        for ic in range(nic):
            rm = self.SM.t[:, 30:31]
            self.op("v", "memset", rm, 0.0, writes=(self.SM,))
            for fb in range(nfb):
                w = min(1024, nf - fb * 1024)
                wf = self.WF.t[:, fb % 2 * 1024:fb % 2 * 1024 + w]
                self.dma("q", wf,
                         w_buf.t[ic * 128:(ic + 1) * 128,
                                 fb * 1024:fb * 1024 + w],
                         reads=(w_buf,), writes=(self.WF,), role="w")
                mx = self.SM.t[:, 29:30]
                self.op("v", "reduce_max", mx, wf, AX.X,
                        apply_absolute_value=True, reads=(self.WF,),
                        writes=(self.SM,))
                self.op("v", "tensor_tensor", rm, rm, mx, ALU.max,
                        writes=(self.SM,))
            self.op("v", "tensor_scalar",
                    self.BIAS.t[:, off + ic:off + ic + 1],
                    rm, 1.0 / 127.0, None, ALU.mult, reads=(self.SM,),
                    writes=(self.BIAS,))
            iv = self.SM.t[:, 31:32]
            self.op("v", "reciprocal", iv, rm, reads=(self.SM,),
                    writes=(self.SM,))
            self.op("v", "tensor_scalar", iv, iv, 127.0, None, ALU.mult,
                    writes=(self.SM,))
            for fb in range(nfb):
                w = min(1024, nf - fb * 1024)
                wf = self.WF.t[:, fb % 2 * 1024:fb % 2 * 1024 + w]
                if nfb > 1:
                    self.dma("q", wf,
                             w_buf.t[ic * 128:(ic + 1) * 128,
                                     fb * 1024:fb * 1024 + w],
                             reads=(w_buf,), writes=(self.WF,), role="w")
                self.quant_tile(wf, wf, iv, (self.WF, self.SM), (self.WF,))
                bq = self.A5.t[:, 7168:7168 + w]
                self.op("v", "tensor_copy", bq, wf, reads=(self.WF,),
                        writes=(self.A5,))
                for fc in range(w // 128):
                    pb = self.npsb()
                    self.op("t", "transpose", pb.t[:, 0:128],
                            bq[:, fc * 128:(fc + 1) * 128], self.IDB.t[:],
                            reads=(self.A5, self.IDB), writes=(pb,))
                    sl = self.OTB.t[:, 3456:3584]
                    self.op("v", "tensor_copy", sl, pb.t[:, 0:128],
                            reads=(pb,), writes=(self.OTB,))
                    gfc = fb * 8 + fc
                    self.dma("q", dst_h.t[gfc * 128:(gfc + 1) * 128,
                                          ic * 128:(ic + 1) * 128],
                             sl, reads=(self.OTB,), writes=(dst_h,),
                             role="o")

    def l0_ffn_gate_up(self):
        self.op("v", "memset", self.XS.t[:, 2049:2050], 0.0,
                writes=(self.XS,))
        for key, dst in (("swg", 34), ("swu", 56)):
            off = self.bias_slots[key]
            nic = 22
            self.op("v", "tensor_scalar", self.SM.t[:, dst:dst + nic],
                    self.BIAS.t[:, off:off + nic], self.cp("sx"), None,
                    ALU.mult, reads=(self.BIAS, self.CP), writes=(self.SM,))
        for tb in range(NQ):
            for c in range(NC):
                self.dma("g", self.A5.t[:, c * 512:(c + 1) * 512],
                         self.KXT.t[c * 128:(c + 1) * 128,
                                    tb * 512:(tb + 1) * 512],
                         reads=(self.KXT,), writes=(self.A5,), role="x")
            for ic in range(22):
                wslot = (ic % 2) * 2048
                for wi, wsrc in enumerate((self.KWGT, self.KWUT)):
                    wap = wsrc.t[:, ic * 128:(ic + 1) * 128]
                    wap = wap.rearrange("(c p) o -> p c o", p=128)
                    self.dma("q",
                             self.WB.t[:, wslot + wi * 1024:
                                       wslot + wi * 1024 + 1024].rearrange(
                                 "p (c o) -> p c o", o=128),
                             wap, reads=(wsrc,), writes=(self.WB,), role="w")
                pg = self.npsf()
                pu = self.npsf()
                for wi, pb in ((0, pg), (1, pu)):
                    calls = []
                    for c in range(NC):
                        calls.append(("matmul",
                                      (pb.t[:, 0:512],
                                       self.WB.t[:, wslot + wi * 1024 + c * 128:
                                                 wslot + wi * 1024 + (c + 1) * 128],
                                       self.A5.t[:, c * 512:(c + 1) * 512]),
                                      dict(start=(c == 0), stop=(c == 7))))
                    self.group("t", calls, reads=(self.WB, self.A5),
                               writes=(pb,))
                sg = self.XS.t[:, 0:512]
                su = self.XS.t[:, 512:1024]
                self.op("s", "activation", sg, pg.t[:, 0:512], AF.Silu,
                        scale=self.SM.t[:, 34 + ic:35 + ic],
                        reads=(pg, self.SM), writes=(self.XS,))
                self.op("s", "activation", su, pu.t[:, 0:512], AF.Identity,
                        scale=self.SM.t[:, 56 + ic:57 + ic],
                        reads=(pu, self.SM), writes=(self.XS,))
                hh = self.XS.t[:, 1024:1536]
                self.op("v", "tensor_tensor", hh, sg, su, ALU.mult,
                        writes=(self.XS,))
                mx = self.SM.t[:, 29:30]
                self.op("v", "reduce_max", mx, hh, AX.X,
                        apply_absolute_value=True, reads=(self.XS,),
                        writes=(self.SM,))
                self.op("v", "tensor_tensor", self.XS.t[:, 2049:2050],
                        self.XS.t[:, 2049:2050], mx, ALU.max,
                        reads=(self.SM,), writes=(self.XS,))
                self.dma("q", self.Hh.t[ic * 128:(ic + 1) * 128,
                                        tb * 512:(tb + 1) * 512],
                         hh, reads=(self.XS,), writes=(self.Hh,), role="o")
        self.part_max(self.XS.t[:, 2049:2050], self.SM.t[0:1, 0:1])
        self.collective_post(self.SM.t[0:1, 0:1], "gh")

    def l0_ffn_qh_down(self):
        self.collective_read("gh", "gh")
        self._mk_scale(self.SM.t[0:1, 17:18], "gh", "sh")
        for ic in range(22):
            for tb in range(NQ):
                xs = self.XS.t[:, 0:512]
                self.dma("q", xs, self.Hh.t[ic * 128:(ic + 1) * 128,
                                            tb * 512:(tb + 1) * 512],
                         reads=(self.Hh,), writes=(self.XS,), role="x")
                self.quant_tile(xs, xs, self.cp("shi"), (self.XS,),
                                (self.XS,))
                bsl = self.A5.t[:, 7680:8192]
                self.op("v", "tensor_copy", bsl, xs, reads=(self.XS,),
                        writes=(self.A5,))
                self.dma("q", self.KHT.t[ic * 128:(ic + 1) * 128,
                                         tb * 512:(tb + 1) * 512],
                         bsl, reads=(self.A5,), writes=(self.KHT,), role="o")
        offd = self.bias_slots["swd"]
        self.op("v", "tensor_scalar", self.SM.t[:, 34:42],
                self.BIAS.t[:, offd:offd + 8], self.cp("sh"), None,
                ALU.mult, reads=(self.BIAS, self.CP), writes=(self.SM,))
        for tb in range(NQ):
            for ic in range(22):
                self.dma("g", self.A2.t[:, ic * 512:(ic + 1) * 512],
                         self.KHT.t[ic * 128:(ic + 1) * 128,
                                    tb * 512:(tb + 1) * 512],
                         reads=(self.KHT,), writes=(self.A2,), role="x")
            for oc in range(NC):
                wslot = (oc % 2) * 2816
                wap = self.KDT.t[:, oc * 128:(oc + 1) * 128]
                wap = wap.rearrange("(c p) o -> p c o", p=128)
                self.dma("q",
                         self.WB.t[:, wslot:wslot + 2816].rearrange(
                             "p (c o) -> p c o", o=128),
                         wap, reads=(self.KDT,), writes=(self.WB,), role="w")
                pb = self.npsf()
                calls = []
                for ic in range(22):
                    calls.append(("matmul",
                                  (pb.t[:, 0:512],
                                   self.WB.t[:, wslot + ic * 128:
                                             wslot + (ic + 1) * 128],
                                   self.A2.t[:, ic * 512:(ic + 1) * 512]),
                                  dict(start=(ic == 0), stop=(ic == 21))))
                self.group("t", calls, reads=(self.WB, self.A2), writes=(pb,))
                self.evac_f32(self.GT, pb, oc, tb, func=AF.Identity,
                              scale=self.SM.t[:, 34 + oc:35 + oc],
                              scale_buf=self.SM)

    def update(self, l):
        last = (l == 3)
        for tb in range(NQ):
            ps_s = self.PSF[4]
            ps_q = self.PSF[5]
            self.pin(ps_s)
            self.pin(ps_q)
            for c in range(NC):
                xsl = self.XS.t[:, c * 512:(c + 1) * 512]
                self.dma("q", xsl, self.XT.t[c * 128:(c + 1) * 128,
                                             tb * 512:(tb + 1) * 512],
                         reads=(self.XT,), writes=(self.XS,), role="x")
                g0 = self.WF.t[:, 0:512]
                s0 = self.WF.t[:, 512:1024]
                gt_h = self.XIT if l == 1 else self.GT
                self.dma("q", g0, gt_h.t[c * 128:(c + 1) * 128,
                                         tb * 512:(tb + 1) * 512],
                         reads=(gt_h,), writes=(self.WF,), role="x")
                if l == 1:
                    for j in range(4):
                        wf2 = self.WF.t[:, 1024:1152]
                        self.dma("q", wf2,
                                 self.Stok.t[tb * 512 + j * 128:
                                             tb * 512 + (j + 1) * 128,
                                             c * 128:(c + 1) * 128],
                                 reads=(self.Stok,), writes=(self.WF,),
                                 role="x")
                        pb = self.npsf()
                        self.op("t", "transpose", pb.t[:, 0:128], wf2,
                                self.ID32.t[:], reads=(self.WF, self.ID32),
                                writes=(pb,))
                        self.op("v", "tensor_copy",
                                s0[:, j * 128:(j + 1) * 128],
                                pb.t[:, 0:128], reads=(pb,),
                                writes=(self.WF,))
                else:
                    self.dma("q", s0, self.ST.t[c * 128:(c + 1) * 128,
                                                tb * 512:(tb + 1) * 512],
                             reads=(self.ST,), writes=(self.WF,), role="x")
                g1 = self.WF.t[:, 1536:2048]
                self.op("s", "activation", g1, g0, AF.Silu,
                        reads=(self.WF,), writes=(self.WF,))
                self.op("v", "tensor_tensor", g1, g1, s0, ALU.mult,
                        writes=(self.WF,))
                self.op("v", "tensor_tensor", xsl, xsl, g1, ALU.add,
                        reads=(self.WF,), writes=(self.XS,))
                if last:
                    for j in range(4):
                        pb = self.npsf()
                        self.op("t", "transpose", pb.t[:, 0:128],
                                xsl[:, j * 128:(j + 1) * 128],
                                self.ID32.t[:],
                                reads=(self.XS, self.ID32), writes=(pb,))
                        wf2 = self.WF.t[:, 1024:1152]
                        self.op("v", "tensor_copy", wf2, pb.t[:, 0:128],
                                reads=(pb,), writes=(self.WF,))
                        self.dma("q", self.OUT.t[tb * 512 + j * 128:
                                                 tb * 512 + (j + 1) * 128,
                                                 c * 128:(c + 1) * 128],
                                 wf2, reads=(self.WF,), writes=(self.OUT,),
                                 role="o")
                else:
                    self.dma("q", self.XT.t[c * 128:(c + 1) * 128,
                                            tb * 512:(tb + 1) * 512],
                             xsl, reads=(self.XS,), writes=(self.XT,),
                             role="o")
                self.op("t", "matmul", ps_s.t[0:1, 0:512], self.ONEC.t[:],
                        xsl, start=(c == 0), stop=(c == NC - 1),
                        reads=(self.ONEC, self.XS), writes=(ps_s,),
                        skip_group_check=True)
                sqt = self.WF.t[:, 1536:2048]
                self.op("s", "activation", sqt, xsl, AF.Square,
                        reads=(self.XS,), writes=(self.WF,))
                self.op("t", "matmul", ps_q.t[0:1, 0:512], self.ONEC.t[:],
                        sqt, start=(c == 0), stop=(c == NC - 1),
                        reads=(self.ONEC, self.WF), writes=(ps_q,),
                        skip_group_check=True)
            if last:
                self.unpin(ps_s)
                self.unpin(ps_q)
                continue
            mu = self.SM.t[0:1, 0:512]
            rs = self.SM.t[0:1, 512:1024]
            self.op("v", "tensor_scalar", mu, ps_s.t[0:1, 0:512],
                    1.0 / 1024.0, None, ALU.mult, reads=(ps_s,),
                    writes=(self.SM,))
            self.op("v", "tensor_scalar", rs, ps_q.t[0:1, 0:512],
                    1.0 / 1024.0, None, ALU.mult, reads=(ps_q,),
                    writes=(self.SM,))
            self.unpin(ps_s)
            self.unpin(ps_q)
            t0 = self.SM.t[0:1, 0:512]
            tmp = self.WF.t[0:1, 0:512]
            self.op("v", "tensor_tensor", tmp, mu, mu, ALU.mult,
                    reads=(self.SM,), writes=(self.WF,))
            self.op("v", "tensor_tensor", rs, rs, tmp, ALU.subtract,
                    reads=(self.WF,), writes=(self.SM,))
            self.op("v", "tensor_scalar", rs, rs, 1e-5, None, ALU.add,
                    writes=(self.SM,))
            self.op("s", "activation", rs, rs, AF.Sqrt, writes=(self.SM,))
            self.op("v", "reciprocal", rs, rs, writes=(self.SM,))
            pmu = self.npsf()
            prs = self.npsf()
            self.op("t", "matmul", pmu.t[:, 0:512], self.ONER.t[:], mu,
                    start=True, stop=True, reads=(self.ONER, self.SM),
                    writes=(pmu,))
            self.op("t", "matmul", prs.t[:, 0:512], self.ONER.t[:], rs,
                    start=True, stop=True, reads=(self.ONER, self.SM),
                    writes=(prs,))
            mub = self.WF.t[:, 0:512]
            rsb = self.WF.t[:, 512:1024]
            self.op("v", "tensor_copy", mub, pmu.t[:, 0:512], reads=(pmu,),
                    writes=(self.WF,))
            self.op("v", "tensor_copy", rsb, prs.t[:, 0:512], reads=(prs,),
                    writes=(self.WF,))
            for c in range(NC):
                xsl = self.XS.t[:, c * 512:(c + 1) * 512]
                xn = self.WF.t[:, 1024:1536]
                self.op("v", "tensor_tensor", xn, xsl, mub, ALU.subtract,
                        reads=(self.XS, self.WF), writes=(self.WF,))
                self.op("v", "tensor_tensor", xn, xn, rsb, ALU.mult,
                        writes=(self.WF,))
                self.dma("q", self.XNT.t[c * 128:(c + 1) * 128,
                                         tb * 512:(tb + 1) * 512],
                         xn, reads=(self.WF,), writes=(self.XNT,), role="o")
                if l == 0:
                    for j in range(4):
                        pb = self.npsb()
                        bsl = self.A5.t[:, 7680:7808]
                        bsl2 = self.A5.t[:, 7808:7936]
                        self.op("v", "tensor_copy", bsl2,
                                xn[:, j * 128:(j + 1) * 128],
                                reads=(self.WF,), writes=(self.A5,))
                        self.op("t", "transpose", pb.t[:, 0:128], bsl2,
                                self.IDB.t[:],
                                reads=(self.A5, self.IDB), writes=(pb,))
                        self.op("v", "tensor_copy", bsl, pb.t[:, 0:128],
                                reads=(pb,), writes=(self.A5,))
                        self.dma("q",
                                 self.XNtokB.t[tb * 512 + j * 128:
                                               tb * 512 + (j + 1) * 128,
                                               c * 128:(c + 1) * 128],
                                 bsl, reads=(self.A5,), writes=(self.XNtokB,),
                                 role="o")

    def fft_consts(self):
        fcb_map = [("f64c", 0, 32), ("f64s", 64, 32), ("f128c", 128, 128),
                   ("f128sn", 256, 128), ("f128s", 384, 128),
                   ("i128c", 512, 128), ("i128sn", 640, 128),
                   ("i128s", 768, 128), ("ibc", 896, 64), ("ibsn", 928, 64)]
        for nm, col, rows in fcb_map:
            w = HOSTC[nm].shape[1]
            self.dma("q", self.FCB.t[0:rows, col:col + w], self.C[nm].t,
                     reads=(self.C[nm],), writes=(self.FCB,))
        fcf_map = [("twc", 0, 64), ("tws", 128, 64), ("twsn", 256, 64),
                   ("itwc", 384, 128), ("itws", 448, 128),
                   ("itwsn", 512, 128)]
        for nm, col, rows in fcf_map:
            w = HOSTC[nm].shape[1]
            self.dma("q", self.FCF.t[0:rows, col:col + w], self.C[nm].t,
                     reads=(self.C[nm],), writes=(self.FCF,))

    def fft_fwd_s1(self, src_tok, dst_r, dst_i):
        src3 = src_tok.t.rearrange("(a b) d -> a b d", a=32)
        for db in range(NC):
            for half in range(2):
                ap = src3[:, half * 64:(half + 1) * 64,
                          db * 128:(db + 1) * 128]
                self.dma("g",
                         self.A1.t[0:32, 0:8192].rearrange(
                             "p (b d) -> p b d", b=64),
                         ap, reads=(src_tok,), writes=(self.A1,), role="x")
                for nb in range(16):
                    n2b = half * 64 + nb * 4
                    pr = self.npsf()
                    pi_ = self.npsf()
                    rhs = self.A1.t[0:32, nb * 512:(nb + 1) * 512]
                    self.op("t", "matmul", pr.t[0:64, 0:512],
                            self.FCB.t[0:32, 0:64], rhs, start=True,
                            stop=True, reads=(self.FCB, self.A1),
                            writes=(pr,))
                    self.op("t", "matmul", pi_.t[0:64, 0:512],
                            self.FCB.t[0:32, 64:128], rhs, start=True,
                            stop=True, reads=(self.FCB, self.A1),
                            writes=(pi_,))
                    ar = self.XS.t[0:64, 0:512]
                    ai = self.XS.t[0:64, 512:1024]
                    t2 = self.XS.t[0:64, 1024:1536]
                    prv = pr.t[0:64, 0:512].rearrange("p (a d) -> p a d", a=4)
                    piv = pi_.t[0:64, 0:512].rearrange("p (a d) -> p a d", a=4)
                    arv = ar.rearrange("p (a d) -> p a d", a=4)
                    aiv = ai.rearrange("p (a d) -> p a d", a=4)
                    t2v = t2.rearrange("p (a d) -> p a d", a=4)
                    def twbc(col):
                        apx = self.FCF.t[0:64, col + n2b:col + n2b + 4]
                        return apx.unsqueeze(2).broadcast_to([64, 4, 128])
                    self.op("v", "tensor_tensor", arv, prv, twbc(0),
                            ALU.mult, reads=(pr, self.FCF),
                            writes=(self.XS,))
                    self.op("v", "tensor_tensor", t2v, piv, twbc(256),
                            ALU.mult, reads=(pi_, self.FCF),
                            writes=(self.XS,))
                    self.op("v", "tensor_tensor", ar, ar, t2, ALU.add,
                            writes=(self.XS,))
                    self.op("v", "tensor_tensor", aiv, prv, twbc(128),
                            ALU.mult, reads=(pr, self.FCF),
                            writes=(self.XS,))
                    self.op("v", "tensor_tensor", t2v, piv, twbc(0),
                            ALU.mult, reads=(pi_, self.FCF),
                            writes=(self.XS,))
                    self.op("v", "tensor_tensor", ai, ai, t2, ALU.add,
                            writes=(self.XS,))
                    for dst, srcap in ((dst_r, ar), (dst_i, ai)):
                        dap = dst.t.rearrange("n (k d) -> k n d", k=64)
                        dap = dap[:, n2b:n2b + 4, db * 128:(db + 1) * 128]
                        self.dma("q", dap,
                                 srcap.rearrange("p (a d) -> p a d", a=4),
                                 reads=(self.XS,), writes=(dst,), role="o")

    def fft_t_s2(self):
        for blk in range(128):
            k1 = blk // 2
            off = blk * 512
            gr = self.A1.t[:, 0:512]
            gi = self.A1.t[:, 512:1024]
            self.dma("g", gr, self.Gr.t[:, off:off + 512],
                     reads=(self.Gr,), writes=(self.A1,), role="x")
            self.dma("g", gi, self.Gi.t[:, off:off + 512],
                     reads=(self.Gi,), writes=(self.A1,), role="x")
            pxr = self.npsf()
            pxi = self.npsf()
            self.group("t", [
                ("matmul", (pxr.t[:, 0:512], self.FCB.t[:, 128:256], gr),
                 dict(start=True, stop=False)),
                ("matmul", (pxr.t[:, 0:512], self.FCB.t[:, 256:384], gi),
                 dict(start=False, stop=True))],
                reads=(self.FCB, self.A1), writes=(pxr,))
            self.group("t", [
                ("matmul", (pxi.t[:, 0:512], self.FCB.t[:, 384:512], gr),
                 dict(start=True, stop=False)),
                ("matmul", (pxi.t[:, 0:512], self.FCB.t[:, 128:256], gi),
                 dict(start=False, stop=True))],
                reads=(self.FCB, self.A1), writes=(pxi,))
            for pb, dst in ((pxr, self.TFr), (pxi, self.TFi)):
                sl = self.XS.t[:, 0:512]
                self.op("v", "tensor_copy", sl, pb.t[:, 0:512],
                        reads=(pb,), writes=(self.XS,))
                self.dma("q", dst.t[:, off:off + 512], sl,
                         reads=(self.XS,), writes=(dst,), role="o")

    def fft_x_main(self):
        for blk in range(128):
            k1 = blk // 2
            off = blk * 512
            gr = self.A1.t[:, 0:512]
            gi = self.A1.t[:, 512:1024]
            self.dma("g", gr, self.Gr.t[:, off:off + 512],
                     reads=(self.Gr,), writes=(self.A1,), role="x")
            self.dma("g", gi, self.Gi.t[:, off:off + 512],
                     reads=(self.Gi,), writes=(self.A1,), role="x")
            pxr = self.npsf()
            pxi = self.npsf()
            self.group("t", [
                ("matmul", (pxr.t[:, 0:512], self.FCB.t[:, 128:256], gr),
                 dict(start=True, stop=False)),
                ("matmul", (pxr.t[:, 0:512], self.FCB.t[:, 256:384], gi),
                 dict(start=False, stop=True))],
                reads=(self.FCB, self.A1), writes=(pxr,))
            self.group("t", [
                ("matmul", (pxi.t[:, 0:512], self.FCB.t[:, 384:512], gr),
                 dict(start=True, stop=False)),
                ("matmul", (pxi.t[:, 0:512], self.FCB.t[:, 128:256], gi),
                 dict(start=False, stop=True))],
                reads=(self.FCB, self.A1), writes=(pxi,))
            tfr = self.XS.t[:, 0:512]
            tfi = self.XS.t[:, 512:1024]
            self.dma("q", tfr, self.TFr.t[:, off:off + 512],
                     reads=(self.TFr,), writes=(self.XS,), role="x")
            self.dma("q", tfi, self.TFi.t[:, off:off + 512],
                     reads=(self.TFi,), writes=(self.XS,), role="x")
            t1 = self.XS.t[:, 1024:1536]
            t2 = self.XS.t[:, 1536:2048]
            ppr = self.A1.t[:, 1024:1536]
            ppi = self.A1.t[:, 1536:2048]
            self.op("v", "tensor_tensor", t1, pxr.t[:, 0:512], tfr,
                    ALU.mult, reads=(pxr, self.XS), writes=(self.XS,))
            self.op("v", "tensor_tensor", t2, pxi.t[:, 0:512], tfi,
                    ALU.mult, reads=(pxi, self.XS), writes=(self.XS,))
            self.op("v", "tensor_tensor", ppr, t1, t2, ALU.subtract,
                    reads=(self.XS,), writes=(self.A1,))
            self.op("v", "tensor_tensor", t1, pxr.t[:, 0:512], tfi,
                    ALU.mult, reads=(pxr, self.XS), writes=(self.XS,))
            self.op("v", "tensor_tensor", t2, pxi.t[:, 0:512], tfr,
                    ALU.mult, reads=(pxi, self.XS), writes=(self.XS,))
            self.op("v", "tensor_tensor", ppi, t1, t2, ALU.add,
                    reads=(self.XS,), writes=(self.A1,))
            par = self.npsf()
            pai = self.npsf()
            self.group("t", [
                ("matmul", (par.t[:, 0:512], self.FCB.t[:, 512:640], ppr),
                 dict(start=True, stop=False)),
                ("matmul", (par.t[:, 0:512], self.FCB.t[:, 640:768], ppi),
                 dict(start=False, stop=True))],
                reads=(self.FCB, self.A1), writes=(par,))
            self.group("t", [
                ("matmul", (pai.t[:, 0:512], self.FCB.t[:, 768:896], ppr),
                 dict(start=True, stop=False)),
                ("matmul", (pai.t[:, 0:512], self.FCB.t[:, 512:640], ppi),
                 dict(start=False, stop=True))],
                reads=(self.FCB, self.A1), writes=(pai,))
            ar2 = self.XS.t[:, 1024:1536]
            ai2 = self.XS.t[:, 1536:2048]
            itc = self.FCF.t[:, 384 + k1:384 + k1 + 1]
            its = self.FCF.t[:, 448 + k1:448 + k1 + 1]
            itsn = self.FCF.t[:, 512 + k1:512 + k1 + 1]
            self.op("v", "tensor_scalar", ar2, par.t[:, 0:512], itc, None,
                    ALU.mult, reads=(par, self.FCF), writes=(self.XS,))
            self.op("v", "scalar_tensor_tensor", ar2, pai.t[:, 0:512],
                    itsn, ar2, ALU.mult, ALU.add,
                    reads=(pai, self.FCF), writes=(self.XS,))
            self.op("v", "tensor_scalar", ai2, par.t[:, 0:512], its, None,
                    ALU.mult, reads=(par, self.FCF), writes=(self.XS,))
            self.op("v", "scalar_tensor_tensor", ai2, pai.t[:, 0:512],
                    itc, ai2, ALU.mult, ALU.add,
                    reads=(pai, self.FCF), writes=(self.XS,))
            for dst, srcap in ((self.H2r, ar2), (self.H2i, ai2)):
                dap = dst.t[k1:k1 + 1, :].rearrange(
                    "a (n d) -> (a n) d", n=128)[:, (blk % 2) * 512:
                                                 (blk % 2) * 512 + 512]
                self.dma("q", dap, srcap, reads=(self.XS,), writes=(dst,),
                         role="o")

    def fft_x_s3(self):
        dst3 = self.Stok.t.rearrange("(a b) d -> a b d", a=32)
        for blk in range(256):
            n2 = blk // 2
            off = n2 * 1024 + (blk % 2) * 512
            hr = self.A1.t[0:64, 0:512]
            hi = self.A1.t[0:64, 512:1024]
            self.dma("g", hr, self.H2r.t[:, off:off + 512],
                     reads=(self.H2r,), writes=(self.A1,), role="x")
            self.dma("g", hi, self.H2i.t[:, off:off + 512],
                     reads=(self.H2i,), writes=(self.A1,), role="x")
            pb = self.npsf()
            self.group("t", [
                ("matmul", (pb.t[0:32, 0:512], self.FCB.t[0:64, 896:928], hr),
                 dict(start=True, stop=False)),
                ("matmul", (pb.t[0:32, 0:512], self.FCB.t[0:64, 928:960], hi),
                 dict(start=False, stop=True))],
                reads=(self.FCB, self.A1), writes=(pb,))
            sl = self.XS.t[0:32, 0:512]
            self.op("v", "tensor_copy", sl, pb.t[0:32, 0:512],
                    reads=(pb,), writes=(self.XS,))
            self.dma("q", dst3[:, n2, (blk % 2) * 512:(blk % 2) * 512 + 512],
                     sl, reads=(self.XS,), writes=(self.Stok,), role="o")

    def l1_attn(self):
        steps = [self.fft_consts,
                 lambda: self.fft_fwd_s1(self.W["L1_attn_t"], self.Gr,
                                         self.Gi),
                 self.fft_t_s2,
                 lambda: self.fft_fwd_s1(self.XNtokB, self.Gr, self.Gi),
                 self.fft_x_main, self.fft_x_s3]
        for s in steps[:FFT_STEPS]:
            s()

    def shift_pass(self, src_h, dst_h, dst_f32=None):
        for c in range(NC):
            for tb in range(NQ):
                sl = self.A5.t[:, 7680:8192]
                if c < 4 or tb > 0:
                    lo = tb * 512 - (0 if c < 4 else 1)
                    self.dma("q", sl, src_h.t[c * 128:(c + 1) * 128,
                                              lo:lo + 512],
                             reads=(src_h,), writes=(self.A5,), role="x")
                else:
                    self.op("v", "memset", sl[:, 0:1], 0.0,
                            writes=(self.A5,))
                    self.dma("q", sl[:, 1:512],
                             src_h.t[c * 128:(c + 1) * 128, 0:511],
                             reads=(src_h,), writes=(self.A5,), role="x")
                self.dma("q", dst_h.t[c * 128:(c + 1) * 128,
                                      tb * 512:(tb + 1) * 512],
                         sl, reads=(self.A5,), writes=(dst_h,), role="o")
                if dst_f32 is not None:
                    xf = self.XS.t[:, 3584:4096]
                    self.op("v", "tensor_copy", xf, sl, reads=(self.A5,),
                            writes=(self.XS,))
                    self.dma("q", dst_f32.t[c * 128:(c + 1) * 128,
                                            tb * 512:(tb + 1) * 512],
                             xf, reads=(self.XS,), writes=(dst_f32,),
                             role="o")

    def rope_block(self, q0, cos_ap, sin_ap, out_bf_ap, rbufs, obuf):
        for a, b in ((0, 32), (64, 96)):
            self.dma("q", self.QSW.t[a:a + 32, 0:512], q0[b:b + 32, 0:512],
                     reads=rbufs, writes=(self.QSW,))
            self.dma("q", self.QSW.t[b:b + 32, 0:512], q0[a:a + 32, 0:512],
                     reads=rbufs, writes=(self.QSW,))
        t1 = self.QSW.t[:, 512:1024]
        self.op("v", "tensor_tensor", t1, q0, cos_ap, ALU.mult,
                reads=rbufs + (self.TRIG,), writes=(self.QSW,))
        self.op("v", "tensor_tensor", self.QSW.t[:, 0:512],
                self.QSW.t[:, 0:512], sin_ap, ALU.mult,
                reads=(self.TRIG,), writes=(self.QSW,))
        self.op("v", "tensor_tensor", out_bf_ap, t1, self.QSW.t[:, 0:512],
                ALU.add, reads=(self.QSW,), writes=(obuf,))

    def l1_qkv(self, l):
        W = self.W
        wq = W["L1_ffn_wqkv"].t[l:l + 1, :, :].rearrange("a b c -> (a b) c")
        for nm, col in (("L1_ffn_q_scale", 60), ("L1_ffn_k_scale", 61)):
            src = W[nm].t[l:l + 1, :].rearrange("a d -> d a")
            self.dma("q", self.BIAS2.t[0:64, col:col + 1], src,
                     reads=(W[nm],), writes=(self.BIAS2,))
            self.dma("q", self.BIAS2.t[64:128, col:col + 1], src,
                     reads=(W[nm],), writes=(self.BIAS2,))
        self.dma("q", self.SEL.t[:, 0:2], self.C["ind2f"].t,
                 reads=(self.C["ind2f"],), writes=(self.SEL,))
        self.dma("q", self.SEL.t[0:2, 128:256], self.C["sel2f"].t,
                 reads=(self.C["sel2f"],), writes=(self.SEL,))

        def evac(pb, oc, tb):
            isq = oc < 4
            q0 = self.XS.t[:, 0:512]
            self.op("v", "tensor_copy", q0, pb.t[:, 0:512], reads=(pb,),
                    writes=(self.XS,))
            sq = self.WF.t[:, 0:512]
            self.op("s", "activation", sq, q0, AF.Square, reads=(self.XS,),
                    writes=(self.WF,))
            pn = self.npsf()
            self.op("t", "matmul", pn.t[0:2, 0:512], self.SEL.t[:, 0:2],
                    sq, start=True, stop=True, reads=(self.SEL, self.WF),
                    writes=(pn,))
            rn = self.SM.t[0:2, 0:512]
            self.op("s", "activation", rn, pn.t[0:2, 0:512], AF.Sqrt,
                    reads=(pn,), writes=(self.SM,))
            self.op("v", "reciprocal", rn, rn, writes=(self.SM,))
            self.op("v", "tensor_scalar", rn, rn, 1e12, None, ALU.min,
                    writes=(self.SM,))
            pbc = self.npsf()
            self.op("t", "matmul", pbc.t[:, 0:512], self.SEL.t[0:2, 128:256],
                    rn, start=True, stop=True, reads=(self.SEL, self.SM),
                    writes=(pbc,))
            bc = self.WF.t[:, 512:1024]
            self.op("v", "tensor_copy", bc, pbc.t[:, 0:512], reads=(pbc,),
                    writes=(self.WF,))
            self.op("v", "tensor_tensor", q0, q0, bc, ALU.mult,
                    reads=(self.WF,), writes=(self.XS,))
            scol = self.BIAS2.t[:, 60:61] if isq else self.BIAS2.t[:, 61:62]
            self.op("v", "tensor_scalar", q0, q0, scol, None, ALU.mult,
                    reads=(self.BIAS2,), writes=(self.XS,))
            to = 0 if isq else 1024
            out = self.OTB.t[:, 3584:4096]
            self.rope_block(q0, self.TRIG.t[:, to:to + 512],
                            self.TRIG.t[:, to + 512:to + 1024], out,
                            (self.XS,), self.OTB)
            dst = self.QTh if isq else self.KTh
            self.dma("q", dst.t[(oc % 4) * 128:(oc % 4) * 128 + 128,
                                tb * 512:(tb + 1) * 512],
                     out, reads=(self.OTB,), writes=(dst,), role="o")

        def stage_rhs(tb):
            for nm, co in (("xq_cos", 0), ("xq_sin", 512),
                           ("xk_cos", 1024), ("xk_sin", 1536)):
                self.dma("q", self.TRIG.t[:, co:co + 512],
                         self.C[nm].t[:, tb * 512:(tb + 1) * 512],
                         reads=(self.C[nm],), writes=(self.TRIG,))
            for c in range(NC):
                self.dma("g", self.A5.t[:, c * 512:(c + 1) * 512],
                         self.HSH.t[c * 128:(c + 1) * 128,
                                    tb * 512:(tb + 1) * 512],
                         reads=(self.HSH,), writes=(self.A5,), role="x")

        self.proj(self.HSH, self.W["L1_ffn_wqkv"], wq[:, 0:1024], D, 1024,
                  None, None, evac, stage_rhs=stage_rhs)
        self.tok_mm(self.HSH, 0, self.W["L1_ffn_wqkv"], wq[:, 1024:1536],
                    D, 512, None, self.Vh)

    def l1_ffn_block(self, l):
        W = self.W
        self.shift_pass(self.XIT, self.HSH, None)
        self.l1_qkv(l)
        masks = self.stage_masks([f"m_l1_{r}" for r in range(8)])

        def jts(q):
            return range(max(0, 4 * q - 4), 4 * q + 4)

        def mof(jt, q):
            return masks[f"m_l1_{jt - 4 * q + 4}"]

        self.attn_core(self.QTh, self.KTh, self.Vh, self.OTh, 8, "c8",
                       jts, mof)
        wo = W["L1_ffn_wo"].t[l:l + 1, :, :].rearrange("a b c -> (a b) c")

        def evac_add(pb, oc, tb):
            old = self.WF.t[:, 0:512]
            self.dma("q", old, self.XIT.t[oc * 128:(oc + 1) * 128,
                                          tb * 512:(tb + 1) * 512],
                     reads=(self.XIT,), writes=(self.WF,), role="x")
            ns = self.XS.t[:, 3584:4096]
            self.op("v", "tensor_tensor", ns, pb.t[:, 0:512], old, ALU.add,
                    reads=(pb, self.WF), writes=(self.XS,))
            self.dma("q", self.XIT.t[oc * 128:(oc + 1) * 128,
                                     tb * 512:(tb + 1) * 512],
                     ns, reads=(self.XS,), writes=(self.XIT,), role="o")

        self.proj(self.OTh, W["L1_ffn_wo"], wo, 512, D, None, None,
                  evac_add)
        self.shift_pass(self.XIT, self.HSH, self.HSHF)
        self.load_bias_cols(Buf(W["L1_ffn_ln_g"].t[l:l + 1, :].rearrange(
            "a d -> (a d)"), "lg"), D, "lg" + str(l))
        self.load_bias_cols(Buf(W["L1_ffn_ln_b"].t[l:l + 1, :].rearrange(
            "a d -> (a d)"), "lb"), D, "lb" + str(l))
        w1 = W["L1_ffn_w1"].t[l:l + 1, :, :].rearrange("a b c -> (a b) c")
        w2 = W["L1_ffn_w2"].t[l:l + 1, :, :].rearrange("a b c -> (a b) c")
        b1 = W["L1_ffn_b1"].t[l:l + 1, :].rearrange("a d -> a d")
        self.dma("q", self.A2.t[0:1, 11264:14080], b1[:, 0:2816],
                 reads=(W["L1_ffn_b1"],), writes=(self.A2,))
        self.dma("q", self.A2.t[0:1, 14080:16896], b1[:, 2816:5632],
                 reads=(W["L1_ffn_b1"],), writes=(self.A2,))
        b2 = W["L1_ffn_b2"].t[l:l + 1, :].rearrange("a d -> a d")
        self.dma("q", self.A2.t[0:1, 16896:17920], b2,
                 reads=(W["L1_ffn_b2"],), writes=(self.A2,))
        for tb in range(NQ):
            ps_s = self.PSF[4]
            ps_q = self.PSF[5]
            self.pin(ps_s)
            self.pin(ps_q)
            for c in range(NC):
                xsl = self.XS.t[:, c * 512:(c + 1) * 512]
                self.dma("q", xsl, self.HSHF.t[c * 128:(c + 1) * 128,
                                               tb * 512:(tb + 1) * 512],
                         reads=(self.HSHF,), writes=(self.XS,), role="x")
                self.op("t", "matmul", ps_s.t[0:1, 0:512], self.ONEC.t[:],
                        xsl, start=(c == 0), stop=(c == NC - 1),
                        reads=(self.ONEC, self.XS), writes=(ps_s,),
                        skip_group_check=True)
                sqt = self.WF.t[:, 1536:2048]
                self.op("s", "activation", sqt, xsl, AF.Square,
                        reads=(self.XS,), writes=(self.WF,))
                self.op("t", "matmul", ps_q.t[0:1, 0:512], self.ONEC.t[:],
                        sqt, start=(c == 0), stop=(c == NC - 1),
                        reads=(self.ONEC, self.WF), writes=(ps_q,),
                        skip_group_check=True)
            mu = self.SM.t[0:1, 0:512]
            rs = self.SM.t[0:1, 512:1024]
            self.op("v", "tensor_scalar", mu, ps_s.t[0:1, 0:512],
                    1.0 / 1024.0, None, ALU.mult, reads=(ps_s,),
                    writes=(self.SM,))
            self.op("v", "tensor_scalar", rs, ps_q.t[0:1, 0:512],
                    1.0 / 1024.0, None, ALU.mult, reads=(ps_q,),
                    writes=(self.SM,))
            self.unpin(ps_s)
            self.unpin(ps_q)
            tmp = self.WF.t[0:1, 0:512]
            self.op("v", "tensor_tensor", tmp, mu, mu, ALU.mult,
                    reads=(self.SM,), writes=(self.WF,))
            self.op("v", "tensor_tensor", rs, rs, tmp, ALU.subtract,
                    reads=(self.WF,), writes=(self.SM,))
            self.op("v", "tensor_scalar", rs, rs, 1e-5, None, ALU.add,
                    writes=(self.SM,))
            self.op("s", "activation", rs, rs, AF.Sqrt, writes=(self.SM,))
            self.op("v", "reciprocal", rs, rs, writes=(self.SM,))
            pmu = self.npsf()
            prs = self.npsf()
            self.op("t", "matmul", pmu.t[:, 0:512], self.ONER.t[:], mu,
                    start=True, stop=True, reads=(self.ONER, self.SM),
                    writes=(pmu,))
            self.op("t", "matmul", prs.t[:, 0:512], self.ONER.t[:], rs,
                    start=True, stop=True, reads=(self.ONER, self.SM),
                    writes=(prs,))
            mub = self.WF.t[:, 0:512]
            rsb = self.WF.t[:, 512:1024]
            self.op("v", "tensor_copy", mub, pmu.t[:, 0:512], reads=(pmu,),
                    writes=(self.WF,))
            self.op("v", "tensor_copy", rsb, prs.t[:, 0:512], reads=(prs,),
                    writes=(self.WF,))
            for c in range(NC):
                xsl = self.XS.t[:, c * 512:(c + 1) * 512]
                xn = self.WF.t[:, 1024:1536]
                self.op("v", "tensor_tensor", xn, xsl, mub, ALU.subtract,
                        reads=(self.XS, self.WF), writes=(self.WF,))
                self.op("v", "tensor_tensor", xn, xn, rsb, ALU.mult,
                        writes=(self.WF,))
                self.op("v", "tensor_scalar", xn, xn,
                        self.bias_col("lg" + str(l), c), None, ALU.mult,
                        reads=(self.BIAS,), writes=(self.WF,))
                self.op("v", "tensor_scalar",
                        self.A5.t[:, c * 512:(c + 1) * 512], xn,
                        self.bias_col("lb" + str(l), c), None, ALU.add,
                        reads=(self.WF, self.BIAS), writes=(self.A5,))
            for ac in range(22):
                wslot = (ac % 2) * 2048
                for wi, co in ((0, ac * 128), (1, 2816 + ac * 128)):
                    wap = w1[:, co:co + 128].rearrange(
                        "(c p) o -> p c o", p=128)
                    self.dma("q",
                             self.WB.t[:, wslot + wi * 1024:
                                       wslot + wi * 1024 + 1024].rearrange(
                                 "p (c o) -> p c o", o=128),
                             wap, reads=(W["L1_ffn_w1"],),
                             writes=(self.WB,), role="w")
                pa = self.npsf()
                pg = self.npsf()
                for wi, pb in ((0, pa), (1, pg)):
                    calls = []
                    for c in range(NC):
                        calls.append(("matmul",
                                      (pb.t[:, 0:512],
                                       self.WB.t[:, wslot + wi * 1024 + c * 128:
                                                 wslot + wi * 1024 + (c + 1) * 128],
                                       self.A5.t[:, c * 512:(c + 1) * 512]),
                                      dict(start=(c == 0), stop=False)))
                    boff = 11264 + wi * 2816 + ac * 128
                    calls.append(("matmul",
                                  (pb.t[:, 0:512],
                                   self.A2.t[0:1, boff:boff + 128],
                                   self.ONE512B.t[:]),
                                  dict(start=False, stop=True)))
                    self.group("t", calls,
                               reads=(self.WB, self.A5, self.A2,
                                      self.ONE512B), writes=(pb,))
                ge = self.WF.t[:, 0:512]
                self.op("s", "activation", ge, pg.t[:, 0:512], AF.Gelu,
                        reads=(pg,), writes=(self.WF,))
                self.op("v", "tensor_tensor",
                        self.A2.t[:, ac * 512:(ac + 1) * 512],
                        pa.t[:, 0:512], ge, ALU.mult,
                        reads=(pa, self.WF), writes=(self.A2,))
            for oc in range(NC):
                wslot = 4096
                wap = w2[:, oc * 128:(oc + 1) * 128].rearrange(
                    "(c p) o -> p c o", p=128)
                self.dma("q",
                         self.WB.t[:, wslot:wslot + 2816].rearrange(
                             "p (c o) -> p c o", o=128),
                         wap, reads=(W["L1_ffn_w2"],), writes=(self.WB,),
                         role="w")
                pb = self.npsf()
                calls = []
                for ic in range(22):
                    calls.append(("matmul",
                                  (pb.t[:, 0:512],
                                   self.WB.t[:, wslot + ic * 128:
                                             wslot + (ic + 1) * 128],
                                   self.A2.t[:, ic * 512:(ic + 1) * 512]),
                                  dict(start=(ic == 0), stop=False)))
                calls.append(("matmul",
                              (pb.t[:, 0:512],
                               self.A2.t[0:1, 16896 + oc * 128:
                                         16896 + (oc + 1) * 128],
                               self.ONE512B.t[:]),
                              dict(start=False, stop=True)))
                self.group("t", calls,
                           reads=(self.WB, self.A2, self.ONE512B),
                           writes=(pb,))
                old = self.WF.t[:, 0:512]
                self.dma("q", old, self.XIT.t[oc * 128:(oc + 1) * 128,
                                              tb * 512:(tb + 1) * 512],
                         reads=(self.XIT,), writes=(self.WF,), role="x")
                ns = self.XS.t[:, 3584:4096]
                self.op("v", "tensor_tensor", ns, pb.t[:, 0:512], old,
                        ALU.add, reads=(pb, self.WF), writes=(self.XS,))
                self.dma("q", self.XIT.t[oc * 128:(oc + 1) * 128,
                                         tb * 512:(tb + 1) * 512],
                         ns, reads=(self.XS,), writes=(self.XIT,),
                         role="o")

    def l1_ffn(self):
        self.dma("q", self.XIT.t[:, :], self.XNT.t[:, :],
                 reads=(self.XNT,), writes=(self.XIT,), role="m")
        for l in range(2):
            self.l1_ffn_block(l)

    def l2_wcast(self):
        for src, dst in ((self.W["L2_attn_qw"], self.WQB),
                         (self.W["L2_attn_kw"], self.WKB),
                         (self.W["L2_attn_vw"], self.WVB),
                         (self.W["L2_attn_ow"], self.WOB)):
            for c in range(NC):
                self.dma("q", self.HF.t[:], src.t[c * 128:(c + 1) * 128, :],
                         reads=(src,), writes=(self.HF,), role="w")
                sl = self.A5.t[:, 0:1024]
                self.op("v", "tensor_copy", sl, self.HF.t[:],
                        reads=(self.HF,), writes=(self.A5,))
                self.dma("q", dst.t[c * 128:(c + 1) * 128, :], sl,
                         reads=(self.A5,), writes=(dst,), role="o")

    def l2_mform(self):
        for db in range(2):
            for nm, co in (("l2_Er", 0), ("l2_Ei", 4096)):
                for c in range(NC):
                    self.dma("g", self.A5.t[:, co + c * 512:co + (c + 1) * 512],
                             self.C[nm].t[c * 128:(c + 1) * 128,
                                          db * 512:(db + 1) * 512],
                             reads=(self.C[nm],), writes=(self.A5,),
                             role="x")
            for cb in range(NC):
                for wi, wsrc in enumerate(("L2_ffn_w_freq_real",
                                           "L2_ffn_w_freq_imag")):
                    wap = self.W[wsrc].t[:, cb * 128:(cb + 1) * 128]
                    wap = wap.rearrange("(c p) o -> p c o", p=128)
                    self.dma("q",
                             self.WB.t[:, wi * 1024:wi * 1024 + 1024]
                             .rearrange("p (c o) -> p c o", o=128),
                             wap, reads=(self.W[wsrc],), writes=(self.WB,),
                             role="w")
                ps = [self.npsf() for _ in range(4)]
                for pi_, (wi, eo) in enumerate(((0, 0), (1, 4096),
                                                (0, 4096), (1, 0))):
                    calls = []
                    for c in range(NC):
                        calls.append(("matmul",
                                      (ps[pi_].t[:, 0:512],
                                       self.WB.t[:, wi * 1024 + c * 128:
                                                 wi * 1024 + (c + 1) * 128],
                                       self.A5.t[:, eo + c * 512:
                                                 eo + (c + 1) * 512]),
                                      dict(start=(c == 0), stop=(c == 7))))
                    self.group("t", calls, reads=(self.WB, self.A5),
                               writes=(ps[pi_],))
                tr = self.XS.t[:, 0:512]
                ti = self.XS.t[:, 512:1024]
                w1s = self.WF.t[:, 0:512]
                w2s = self.WF.t[:, 512:1024]
                self.op("v", "tensor_copy", w1s, ps[1].t[:, 0:512],
                        reads=(ps[1],), writes=(self.WF,))
                self.op("v", "tensor_copy", w2s, ps[3].t[:, 0:512],
                        reads=(ps[3],), writes=(self.WF,))
                self.op("v", "tensor_tensor", tr, ps[0].t[:, 0:512],
                        w1s, ALU.subtract,
                        reads=(ps[0], self.WF), writes=(self.XS,))
                self.op("v", "tensor_tensor", ti, ps[2].t[:, 0:512],
                        w2s, ALU.add,
                        reads=(ps[2], self.WF), writes=(self.XS,))
                self.dma("q", self.ATr.t[cb * 128:(cb + 1) * 128,
                                         db * 512:(db + 1) * 512],
                         tr, reads=(self.XS,), writes=(self.ATr,), role="o")
                self.dma("q", self.ATi.t[cb * 128:(cb + 1) * 128,
                                         db * 512:(db + 1) * 512],
                         ti, reads=(self.XS,), writes=(self.ATi,), role="o")
        for ob in range(2):
            for nm, co in (("l2_Dr", 0), ("l2_Di", 4096)):
                for c in range(NC):
                    self.dma("g", self.A5.t[:, co + c * 512:co + (c + 1) * 512],
                             self.C[nm].t[c * 128:(c + 1) * 128,
                                          ob * 512:(ob + 1) * 512],
                             reads=(self.C[nm],), writes=(self.A5,),
                             role="x")
            for dc in range(NC):
                for wi, wsrc in enumerate((self.ATr, self.ATi)):
                    wap = wsrc.t[:, dc * 128:(dc + 1) * 128]
                    wap = wap.rearrange("(c p) o -> p c o", p=128)
                    self.dma("q",
                             self.WB.t[:, wi * 1024:wi * 1024 + 1024]
                             .rearrange("p (c o) -> p c o", o=128),
                             wap, reads=(wsrc,), writes=(self.WB,),
                             role="w")
                pr = self.npsf()
                pi_ = self.npsf()
                for pb, (wi, eo) in ((pr, (0, 0)), (pi_, (1, 4096))):
                    calls = []
                    for c in range(NC):
                        calls.append(("matmul",
                                      (pb.t[:, 0:512],
                                       self.WB.t[:, wi * 1024 + c * 128:
                                                 wi * 1024 + (c + 1) * 128],
                                       self.A5.t[:, eo + c * 512:
                                                 eo + (c + 1) * 512]),
                                      dict(start=(c == 0), stop=(c == 7))))
                    self.group("t", calls, reads=(self.WB, self.A5),
                               writes=(pb,))
                tm = self.OTB.t[:, 3584:4096]
                tf = self.XS.t[:, 0:512]
                w1s = self.WF.t[:, 0:512]
                self.op("v", "tensor_copy", w1s, pi_.t[:, 0:512],
                        reads=(pi_,), writes=(self.WF,))
                self.op("v", "tensor_tensor", tf, pr.t[:, 0:512],
                        w1s, ALU.subtract, reads=(pr, self.WF),
                        writes=(self.XS,))
                self.op("v", "tensor_copy", tm, tf, reads=(self.XS,),
                        writes=(self.OTB,))
                self.dma("q", self.Mh.t[dc * 128:(dc + 1) * 128,
                                        ob * 512:(ob + 1) * 512],
                         tm, reads=(self.OTB,), writes=(self.Mh,), role="o")

    def l2_qkv(self):
        for wsrc, dst, rope in ((self.WQB, self.QTh, True),
                                (self.WKB, self.KTh, True)):
            def stage_rhs(tb, rope=rope):
                for nm, co in (("rope_cos", 0), ("rope_sin", 512)):
                    self.dma("q", self.TRIG.t[:, co:co + 512],
                             self.C[nm].t[:, tb * 512:(tb + 1) * 512],
                             reads=(self.C[nm],), writes=(self.TRIG,))
                for c in range(NC):
                    self.dma("g", self.A5.t[:, c * 512:(c + 1) * 512],
                             self.XNT.t[c * 128:(c + 1) * 128,
                                        tb * 512:(tb + 1) * 512],
                             reads=(self.XNT,), writes=(self.A5,), role="x")

            def evac(pb, oc, tb, dst=dst):
                q0 = self.XS.t[:, 0:512]
                self.op("v", "tensor_copy", q0, pb.t[:, 0:512], reads=(pb,),
                        writes=(self.XS,))
                out = self.OTB.t[:, 3584:4096]
                self.rope_block(q0, self.TRIG.t[:, 0:512],
                                self.TRIG.t[:, 512:1024], out,
                                (self.XS,), self.OTB)
                self.dma("q", dst.t[oc * 128:(oc + 1) * 128,
                                    tb * 512:(tb + 1) * 512],
                         out, reads=(self.OTB,), writes=(dst,), role="o")

            self.proj(self.XNT, wsrc, wsrc.t, D, D, None, None, evac,
                      stage_rhs=stage_rhs)
        self.tok_mm(self.XNT, 0, self.WVB, self.WVB.t, D, D, None, self.Vh)

    def l2_attn(self):
        masks = self.stage_masks([f"m_l2_{r}" for r in range(4)])

        def jts(q):
            return range(0, 4 * q + 4)

        def mof(jt, q):
            r = jt - 4 * q
            return masks[f"m_l2_{r}"] if r >= 0 else None

        self.attn_core(self.QTh, self.KTh, self.Vh, self.OTh, 16, "c1d32",
                       jts, mof)
        self.proj(self.OTh, self.WOB, self.WOB.t, D, D, None, None,
                  lambda pb, oc, tb: self.evac_f32(self.ST, pb, oc, tb))

    def l2_ffn(self):
        W = self.W
        self.proj(self.XNT, self.Mh, self.Mh.t, D, D,
                  W["L2_ffn_bias"], W["L2_ffn_bias"].t,
                  lambda pb, oc, tb: self.evac_f32(self.GT, pb, oc, tb))

    def l3_attn(self):
        W = self.W
        wq = W["L3_attn_w_qkv"].t
        bq = W["L3_attn_b_qkv"].t
        self.proj(self.XNT, W["L3_attn_w_qkv"], wq[:, 0:1024], D, D,
                  W["L3_attn_b_qkv"], bq[0:1024],
                  lambda pb, oc, tb: self.evac_bf16(self.QTh, pb, oc, tb))
        self.proj(self.XNT, W["L3_attn_w_qkv"], wq[:, 1024:2048], D, D,
                  W["L3_attn_b_qkv"], bq[1024:2048],
                  lambda pb, oc, tb: self.evac_bf16(self.KTh, pb, oc, tb))
        self.tok_mm(self.XNT, 0, W["L3_attn_w_qkv"], wq[:, 2048:3072],
                    D, D, bq[2048:3072], self.Vh,
                    bias_buf=W["L3_attn_b_qkv"])
        masks = self.stage_masks([f"m_l3_{r}" for r in range(6)])

        def jts(q):
            return range(max(0, 4 * q - 1), min(NT, 4 * q + 5))

        def mof(jt, q):
            return masks[f"m_l3_{jt - 4 * q + 1}"]

        self.attn_core(self.QTh, self.KTh, self.Vh, self.OTh, 16, "c0125",
                       jts, mof)
        self.proj(self.OTh, W["L3_attn_w_out"], W["L3_attn_w_out"].t, D, D,
                  W["L3_attn_b_out"], W["L3_attn_b_out"].t,
                  lambda pb, oc, tb: self.evac_f32(self.ST, pb, oc, tb))

    def l3_ffn(self):
        W = self.W
        for src, dst in ((W["L3_ffn_W_r"], self.WRT),
                         (W["L3_ffn_W_k"], self.WKT),
                         (W["L3_ffn_W_v"], self.WVT)):
            for rc in range(NC):
                sl = self.A5.t[:, 0:1024]
                self.dma("q", sl, src.t[rc * 128:(rc + 1) * 128, :],
                         reads=(src,), writes=(self.A5,), role="w")
                for fc in range(NC):
                    pb = self.npsb()
                    self.op("t", "transpose", pb.t[:, 0:128],
                            sl[:, fc * 128:(fc + 1) * 128], self.IDB.t[:],
                            reads=(self.A5, self.IDB), writes=(pb,))
                    so = self.OTB.t[:, 3456:3584]
                    self.op("v", "tensor_copy", so, pb.t[:, 0:128],
                            reads=(pb,), writes=(self.OTB,))
                    self.dma("q", dst.t[fc * 128:(fc + 1) * 128,
                                        rc * 128:(rc + 1) * 128],
                             so, reads=(self.OTB,), writes=(dst,), role="o")
        self.load_bias_cols(W["L3_ffn_mu_r"], D, "mur")
        self.load_bias_cols(W["L3_ffn_mu_k"], D, "muk")
        for c in range(NC):
            for tb in range(NQ):
                x0 = self.XS.t[:, 0:512]
                xp = self.XS.t[:, 512:1024]
                self.dma("q", x0, self.XNT.t[c * 128:(c + 1) * 128,
                                             tb * 512:(tb + 1) * 512],
                         reads=(self.XNT,), writes=(self.XS,), role="x")
                if tb > 0:
                    self.dma("q", xp,
                             self.XNT.t[c * 128:(c + 1) * 128,
                                        tb * 512 - 1:tb * 512 + 511],
                             reads=(self.XNT,), writes=(self.XS,), role="x")
                else:
                    self.op("v", "memset", xp[:, 0:1], 0.0,
                            writes=(self.XS,))
                    self.dma("q", xp[:, 1:512],
                             self.XNT.t[c * 128:(c + 1) * 128, 0:511],
                             reads=(self.XNT,), writes=(self.XS,), role="x")
                dd = self.XS.t[:, 1024:1536]
                self.op("v", "tensor_tensor", dd, x0, xp, ALU.subtract,
                        writes=(self.XS,))
                for key, dst, so in (("mur", self.XRT, 3584),
                                     ("muk", self.XKT2, 3072)):
                    ob = self.OTB.t[:, so:so + 512]
                    self.op("v", "scalar_tensor_tensor", ob, dd,
                            self.bias_col(key, c), xp, ALU.mult, ALU.add,
                            reads=(self.XS, self.BIAS), writes=(self.OTB,))
                    self.dma("q", dst.t[c * 128:(c + 1) * 128,
                                        tb * 512:(tb + 1) * 512],
                             ob, reads=(self.OTB,), writes=(dst,), role="o")
        self.proj(self.XRT, self.WRT, self.WRT.t, D, D, None, None,
                  lambda pb, oc, tb: self.evac_f32(self.RTf, pb, oc, tb,
                                                   func=AF.Sigmoid))

        def evac_kact(pb, oc, tb):
            t0 = self.XS.t[:, 3072:3584]
            self.op("v", "tensor_scalar", t0, pb.t[:, 0:512], 0.0, None,
                    ALU.max, reads=(pb,), writes=(self.XS,))
            ob = self.OTB.t[:, 3584:4096]
            self.op("s", "activation", ob, t0, AF.Square, reads=(self.XS,),
                    writes=(self.OTB,))
            self.dma("q", self.KATh.t[oc * 128:(oc + 1) * 128,
                                      tb * 512:(tb + 1) * 512],
                     ob, reads=(self.OTB,), writes=(self.KATh,), role="o")

        self.proj(self.XKT2, self.WKT, self.WKT.t, D, D, None, None,
                  evac_kact)

        def evac_g(pb, oc, tb):
            rr = self.WF.t[:, 0:512]
            self.dma("q", rr, self.RTf.t[oc * 128:(oc + 1) * 128,
                                         tb * 512:(tb + 1) * 512],
                     reads=(self.RTf,), writes=(self.WF,), role="x")
            t0 = self.XS.t[:, 3072:3584]
            self.op("v", "tensor_tensor", t0, pb.t[:, 0:512], rr, ALU.mult,
                    reads=(pb, self.WF), writes=(self.XS,))
            self.dma("q", self.GT.t[oc * 128:(oc + 1) * 128,
                                    tb * 512:(tb + 1) * 512],
                     t0, reads=(self.XS,), writes=(self.GT,), role="o")

        self.proj(self.KATh, self.WVT, self.WVT.t, D, D, None, None,
                  evac_g)

    def l0_ffn(self):
        self.l0_ffn_scales()
        self.l0_ffn_qx()
        self.l0_ffn_qw(self.W["L0_ffn_w_gate"], 2816, 1024, self.KWGT, "swg")
        self.l0_ffn_qw(self.W["L0_ffn_w_up"], 2816, 1024, self.KWUT, "swu")
        self.l0_ffn_gate_up()
        self.l0_ffn_qw(self.W["L0_ffn_w_down"], 1024, 2816, self.KDT, "swd")
        self.l0_ffn_qh_down()

    def build(self):
        for v, k in ((0.125, "c0125"), (1.0 / 32.0, "c1d32"), (8.0, "c8")):
            self.op("v", "memset", self.cp(k), v, writes=(self.CP,))
        phases = [
            self.p0_ln,
            self.l0_qkv, self.l0_attn,
            self.l0_ffn_scales, self.l0_ffn_qx,
            lambda: self.l0_ffn_qw(self.W["L0_ffn_w_gate"], 2816, 1024,
                                   self.KWGT, "swg"),
            lambda: self.l0_ffn_qw(self.W["L0_ffn_w_up"], 2816, 1024,
                                   self.KWUT, "swu"),
            self.l0_ffn_gate_up,
            lambda: self.l0_ffn_qw(self.W["L0_ffn_w_down"], 1024, 2816,
                                   self.KDT, "swd"),
            self.l0_ffn_qh_down,
            lambda: self.update(0),
            self.l1_attn, self.l1_ffn,
            lambda: self.update(1),
            self.l2_wcast, self.l2_mform, self.l2_qkv, self.l2_attn,
            self.l2_ffn,
            lambda: self.update(2),
            self.l3_attn, self.l3_ffn,
            lambda: self.update(3),
        ]
        for ph in phases[:N_PHASES]:
            ph()
        for i, nm in enumerate(TAPS):
            src = getattr(self, nm)
            shp = list(src.t.shape)
            dt = src.t.dtype if hasattr(src.t, "dtype") else F32
            tp = Buf(self.nc.dram_tensor(f"tap{i}_{nm}", shp, dt,
                                         kind="ExternalOutput").ap(), nm)
            self.dma("q", tp.t[:], src.t[:], reads=(src,), writes=(tp,),
                     role="o")
        if N_PHASES < 99:
            self.dma("q", self.OUT.t[:], self.X_in.t[:],
                     reads=(self.X_in,), writes=(self.OUT,), role="o")
        self.finish()
        return self.nc

    def finish(self):
        for r in self.droles:
            self._wait("q", self.dsem[r], 16 * self.dcount[r])
        for k in ("t", "v", "s", "g"):
            self._wait("q", self.esem[k], self.ecount[k])


def host_prep(inputs):
    ins = {k: np.asarray(v) for k, v in inputs.items()}
    w1 = ins["L1_ffn_w1"]
    w1p = np.zeros((2, 1024, 5632), dtype=np.float32)
    w1p[:, :, 0:2730] = w1[:, :, 0:2730]
    w1p[:, :, 2816:2816 + 2730] = w1[:, :, 2730:5460]
    b1 = ins["L1_ffn_b1"]
    b1p = np.zeros((2, 5632), dtype=np.float32)
    b1p[:, 0:2730] = b1[:, 0:2730]
    b1p[:, 2816:2816 + 2730] = b1[:, 2730:5460]
    w2 = ins["L1_ffn_w2"]
    w2p = np.zeros((2, 2816, 1024), dtype=np.float32)
    w2p[:, 0:2730, :] = w2
    ins["L1_ffn_w1"] = w1p
    ins["L1_ffn_b1"] = b1p
    ins["L1_ffn_w2"] = w2p
    perm = np.zeros(1024, dtype=np.int64)
    for h in range(16):
        for i in range(32):
            perm[h * 64 + i] = h * 64 + 2 * i
            perm[h * 64 + 32 + i] = h * 64 + 2 * i + 1
    ins["L2_attn_qw"] = np.ascontiguousarray(ins["L2_attn_qw"][:, perm])
    ins["L2_attn_kw"] = np.ascontiguousarray(ins["L2_attn_kw"][:, perm])
    return ins


def kernel(**inputs):
    ins = host_prep(inputs)
    x = np.ascontiguousarray(ins.pop("L0_attn_x"))
    mk = MK()
    nc = mk.build()
    base = {k: np.ascontiguousarray(v) for k, v in ins.items()}
    for nm, arr in HOSTC.items():
        base["c_" + nm] = np.ascontiguousarray(arr)
    in_maps = []
    for b in range(8):
        m = dict(base)
        m["x"] = np.ascontiguousarray(x[b])
        in_maps.append(m)
    res = run_bass_kernel_spmd(nc, in_maps, core_ids=list(range(8)))
    out = np.stack([r["out"] for r in res.results], axis=0)
    if TAPS:
        kernel.taps = [{f"tap{i}_{nm}": r[f"tap{i}_{nm}"]
                        for i, nm in enumerate(TAPS)}
                       for r in res.results]
    return out.astype(np.float32)
